# version 05
# baseline (speedup 1.0000x reference)
import numpy as np
from contextlib import ExitStack

import concourse.bass as bass
import concourse.mybir as mybir
from concourse.bass_utils import run_bass_kernel_spmd

F32 = mybir.dt.float32
BF16 = mybir.dt.bfloat16
AF = mybir.ActivationFunctionType
ALU = mybir.AluOpType
AX = mybir.AxisListType.X

M, D, NCH, DCH = 2048, 1024, 16, 8
NP, FF, NE, NH = 2048, 2048, 8, 16
EPS1, EPS2 = 1e-5, 1e-3
ENGS = ("g", "t", "s", "v", "y")


def ts(i, n=128):
    return slice(i * n, (i + 1) * n)


class Prog:
    def __init__(self, nc, es):
        self.nc, self.es = nc, es
        self.items, self.sems, self.cnt = [], {}, {}
        self.nbar = 0

    def sem(self, name):
        if name not in self.sems:
            self.sems[name] = self.es.enter_context(self.nc.semaphore(name))
            self.cnt[name] = 0
        return self.sems[name]

    def inc(self, name, k=1):
        self.sem(name)
        self.cnt[name] += k
        return self.cnt[name]

    def val(self, name):
        self.sem(name)
        return self.cnt[name]

    def on(self, eng, fn):
        self.items.append((eng, fn))

    def barrier(self):
        bar = self.sem("bar")
        self.nbar += 1
        tgt = 5 * self.nbar
        for e in ENGS:
            def _f(E, tgt=tgt, bar=bar):
                E.drain().then_inc(bar, 1)
                E.wait_ge(bar, tgt)
            self.on(e, _f)

    def replay(self, eng, E):
        w = EngWrap(E)
        for e, f in self.items:
            if e == eng:
                f(w)


class EngWrap:
    _PASS = {"drain", "wait_ge", "nop"}

    def __init__(self, E):
        self.__dict__["_E"] = E
        self.__dict__["last"] = None

    def __getattr__(self, nm):
        a = getattr(self._E, nm)
        if callable(a):
            def w(*args, **kw):
                r = a(*args, **kw)
                if nm not in EngWrap._PASS and hasattr(r, "then_inc"):
                    self.__dict__["last"] = r
                return r
            return w
        return a

    def last_inc(self, sem, n=1):
        last = self.__dict__["last"]
        self.__dict__["last"] = None
        if last is None:
            return self._E.drain().then_inc(sem, n)
        return last.then_inc(sem, n)


def build(taps=(), upto=99):
    nc = bass.Bass("TRN2", target_bir_lowering=False)
    d = {}

    def din(name, shape, dt=F32):
        d[name] = nc.dram_tensor(name, list(shape), dt, kind="ExternalInput").ap()

    def dout(name, shape, dt=F32):
        d[name] = nc.dram_tensor(name, list(shape), dt, kind="ExternalOutput").ap()

    def dint(name, shape, dt=F32):
        d[name] = nc.dram_tensor(name, list(shape), dt).ap()

    din("x", (M, D))
    din("wq", (D, D)); din("wk", (D, D)); din("wv", (D, D)); din("wo", (D, D))
    din("bqkc", (128, 32)); din("brows", (4, D))
    din("gamma_bc", (128, D)); din("beta_bc", (128, D))
    din("phi2", (D, NP)); din("w0", (64, D, D)); din("b0e", (NP, D))
    din("wq1", (D, D)); din("wk1", (D, D)); din("wv1", (D, D)); din("wout", (D, D))
    din("rw", (NE, D)); din("ebias_bc", (128, NE))
    din("swg", (FF, D)); din("swu", (FF, D)); din("swd", (D, FF))
    din("ewg", (NE, FF, D)); din("ewu", (NE, FF, D)); din("ewd", (NE, D, FF))
    dout("out", (M, D))

    dint("E_hbm", (M, NP)); dint("Z_hbm", (M, NP))
    dint("s_hbm", (M, D)); dint("s1_hbm", (M, D)); dint("x1_hbm", (M, D))
    dint("q_hbm", (8, 128, M)); dint("k_hbm", (8, 128, M))
    dint("v_hbm", (8, 128, NCH * 2 * 66))
    dint("slotsT_hbm", (D, NP))
    dint("wg_bf9", (9, FF, D), BF16)
    dint("wu_bf9", (9, FF, D), BF16)
    dint("wd_bf9", (9, D, FF), BF16)

    TAPS = set(taps)
    tapshape = {"xnT": (D, M), "E": (M, NP), "s": (M, D), "x1": (M, D),
                "s1": (M, D), "slotsT": (D, NP), "y": (NP, D),
                "comb": (M, NE), "oacc": (M, D), "xn1T": (D, M)}
    for t in TAPS:
        dout("tap_" + t, tapshape[t])

    with ExitStack() as es:
        p = Prog(nc, es)
        S = p.sem
        sb = es.enter_context

        ident = nc.alloc_sbuf_tensor("ident", [128, 128], F32).ap()
        ones = nc.alloc_sbuf_tensor("ones", [128, 128], BF16).ap()
        onesf = nc.alloc_sbuf_tensor("onesf", [1, 128], F32).ap()
        bqkc = nc.alloc_sbuf_tensor("bqkc_sb", [128, 32], F32).ap()
        brow = nc.alloc_sbuf_tensor("brow", [1, 2, D], BF16).ap()
        ebias = nc.alloc_sbuf_tensor("ebias_sb", [128, NE], F32).ap()
        st0 = nc.alloc_sbuf_tensor("st0", [128, NCH, 2], F32).ap()
        st1 = nc.alloc_sbuf_tensor("st1", [128, NCH, 2], F32).ap()
        sac = nc.alloc_sbuf_tensor("sac", [128, NCH, 2], F32).ap()
        tiny = nc.alloc_sbuf_tensor("tiny", [128, 64], F32).ap()
        comb = nc.alloc_sbuf_tensor("comb", [128, NCH, NE], F32).ap()
        rlograw = nc.alloc_sbuf_tensor("rlograw", [128, NCH, NE], F32).ap()
        rowsc = nc.alloc_sbuf_tensor("rowsc", [1, 4, 512], F32).ap()
        sqd = nc.alloc_sbuf_tensor("sqd", [128, 2, D], BF16).ap()
        wA = nc.alloc_sbuf_tensor("wA", [128, 8, 1024], BF16).ap()
        wB = nc.alloc_sbuf_tensor("wB", [128, 8, 1024], BF16).ap()

        AR = 153 * 1024
        arena_t = nc.alloc_sbuf_tensor("arena", [128, AR], mybir.dt.uint8)
        ar0 = nc.lookup_mloc(arena_t).addr

        def at(name, off_kb, shape, dt):
            h = nc.alloc_sbuf_tensor_at(name, list(shape), dt,
                                        offset=ar0 + int(off_kb * 1024))
            return h.ap()

        xnTf = at("xnTf", 0, [128, DCH, M], F32)
        x_sb = at("x_sb", 64, [128, NCH, D], F32)
        xnf = [at(f"xnf{i}", 128 + 4 * i, [128, D], F32) for i in range(2)]
        phi_f = at("phi_f", 64, [128, DCH, NP], F32)
        Ech = [at(f"Ech{i}", 128 + 8 * i, [128, NP], F32) for i in range(2)]
        wqp = [at(f"wqp{i}", 128 + 4 * i, [128, 8, 128], F32) for i in range(2)]
        vsp = at("vsp", 136, [128, NCH, 2, 66], F32)
        qkev = at("qkev", 64, [128, 2048], F32)
        qTh = [at(f"qTh{i}", 0 + 4 * i, [128, 1024], F32) for i in range(2)]
        kTp = [at(f"kTp{i}", 8 + 8 * i, [128, 2048], F32) for i in range(2)]
        vp = [at(f"vp{i}", 24 + 9 * i, [128, NCH, 2, 66], F32) for i in range(2)]
        attnT0 = [at(f"attnT0{i}", 42 + 8 * i, [128, 2048], F32) for i in range(2)]
        ctxTh = at("ctxTh", 58, [128, DCH, 1024], F32)
        bcf = at("bcf", 90, [128, 2048], F32)
        bvf = at("bvf", 98, [1, 2, D], F32)
        wof = at("wof", 8, [128, DCH, 1024], F32)
        xch3 = at("xch3", 106, [128, D], F32)
        xnch3 = at("xnch3", 110, [128, D], F32)
        ych3 = at("ych3", 114, [128, D], F32)
        zch3 = at("zch3", 118, [128, D], F32)
        sch3 = [at(f"sch3{i}", 122 + 4 * i, [128, D], F32) for i in range(2)]
        gb_sb = at("gb_sb", 130, [128, 2, D], F32)
        xn_tok = at("xn_tok", 0, [128, NCH, D], F32)
        disp = at("disp", 64, [128, NCH, 1024], F32)
        Ezb = [at(f"Ezb{i}", 128 + 4 * i, [128, 1024], F32) for i in range(2)]
        Zzb = [at(f"Zzb{i}", 136 + 4 * i, [128, 1024], F32) for i in range(2)]
        sev = [at(f"sev{i}", 144 + 4 * i, [128, 1024], F32) for i in range(2)]
        w0f = [at(f"w0f{i}", 0 + 32 * i, [128, DCH, 1024], F32) for i in range(2)]
        scol = [at(f"scol{i}", 64 + 1 * i, [128, DCH, 32], F32) for i in range(2)]
        b0q = [at(f"b0q{i}", 68 + 4 * i, [128, D], F32) for i in range(2)]
        y_f = at("y_f", 88, [128, NCH, D], F32)
        Erl = [at(f"Erl{i}", 0 + 8 * i, [128, NP], F32) for i in range(2)]
        combt = [at(f"combt{i}", 16 + 8 * i, [128, NCH, 128], F32) for i in range(2)]
        gch = [at(f"gch{i}", 32 + 4 * i, [128, D], F32) for i in range(2)]
        x4ch = [at(f"x4ch{i}", 40 + 4 * i, [128, D], F32) for i in range(2)]
        s4ch = [at(f"s4ch{i}", 48 + 4 * i, [128, D], F32) for i in range(2)]
        o4ch = [at(f"o4ch{i}", 56 + 4 * i, [128, D], F32) for i in range(2)]
        xnT = at("xnT", 0, [128, DCH, M], BF16)
        x5ch = [at(f"x5ch{i}", 32 + 4 * i, [128, D], F32) for i in range(2)]
        xn5 = [at(f"xn5{i}", 40 + 4 * i, [128, D], F32) for i in range(2)]
        rw_sb = at("rw_sb", 48, [8, D], F32)
        trf32 = at("trf32", 52, [128, DCH, 128], F32)
        rwT = at("rwT", 56, [128, DCH, NE], F32)
        qTg = at("qTg", 32, [128, 4, M], BF16)
        kTg = at("kTg", 48, [128, 4, M], BF16)
        v_att = at("v_att", 64, [128, NCH, NH, 66], BF16)
        attnT = [at(f"attnT{i}", 97 + 4 * i, [128, 2048], BF16) for i in range(2)]
        ctxT = at("ctxT", 105, [128, DCH, M], BF16)
        bcf1 = at("bcf1", 137, [128, 2048], F32)
        sch = [at(f"sch{i}", 145 + 4 * i, [128, D], F32) for i in range(2)]
        xch = [at(f"xch{i}", 0 + 4 * i, [128, D], F32) for i in range(2)]
        ych = [at(f"ych{i}", 8 + 4 * i, [128, D], F32) for i in range(2)]
        zch = [at(f"zch{i}", 16 + 4 * i, [128, D], F32) for i in range(2)]
        xnf2 = [at(f"xnf2{i}", 24 + 4 * i, [128, D], F32) for i in range(2)]
        rlog = at("rlog", 32, [128, NCH, NE], F32)
        rsm = at("rsm", 34, [128, NCH, NE], F32)
        rs2 = at("rs2", 36, [128, NCH, NE], F32)
        rs3 = at("rs3", 38, [128, NCH, NE], F32)
        rs4 = at("rs4", 40, [128, NCH, NE], F32)
        o_acc = at("o_acc", 32, [128, NCH, D], F32)
        hTh = at("hTh", 96, [128, 8, M], BF16)
        wdTh = at("wdTh", 128, [128, 8, D], BF16)
        wdThb = [wdTh, wdTh]
        silu_g = [at(f"silu_g{i}", 144 + 2 * i, [128, 1024], BF16) for i in range(2)]
        g1ch = [at(f"g1ch{i}", 96 + 4 * i, [128, D], F32) for i in range(2)]
        x9ch = [at(f"x9ch{i}", 104 + 4 * i, [128, D], F32) for i in range(2)]
        s9ch = [at(f"s9ch{i}", 112 + 4 * i, [128, D], F32) for i in range(2)]
        o9ch = [at(f"o9ch{i}", 120 + 4 * i, [128, D], F32) for i in range(2)]

        psA = sb(nc.psum_tensor("psA", [128, 2048], F32))
        psB = sb(nc.psum_tensor("psB", [128, 1024], F32))
        psC = sb(nc.psum_tensor("psC", [128, 1024], F32))
        psBC = [psB, psC]

        def p0():
            p.inc("d_const", 16 * 3)
            def g(E):
                E.dma_start(out=bqkc[:, :], in_=d["bqkc"][:, :]).then_inc(S("d_const"), 16)
                E.dma_start(out=brow[0:1, :, :], in_=d["brows"][None, 2:4, :]).then_inc(S("d_const"), 16)
                E.dma_start(out=ebias[:, :], in_=d["ebias_bc"][:, :]).then_inc(S("d_const"), 16)
                E.memset(ident[:, :], 0.0)
                E.memset(ones[:, :], 1.0)
                E.memset(onesf[:, :], 1.0)
                E.drain()
                E.affine_select(out=ident[:, :], in_=ident[:, :],
                                compare_op=ALU.not_equal, fill=1.0,
                                base=0, pattern=[[-1, 128]], channel_multiplier=1)
                E.last_inc(S("c_g"), 1)
            p.on("g", g)
            p.inc("c_g", 1)

        def stats_batched(st, eps):
            p.inc("c_v", 1)
            def v(E, st=st, eps=eps):
                E.tensor_scalar_mul(tiny[:, 0:16], sac[:, :, 0], 1.0 / D)
                E.tensor_scalar_mul(tiny[:, 16:32], sac[:, :, 1], 1.0 / D)
                E.drain()
                E.tensor_tensor(tiny[:, 32:48], tiny[:, 0:16], tiny[:, 0:16], ALU.mult)
                E.drain()
                E.tensor_sub(tiny[:, 32:48], tiny[:, 16:32], tiny[:, 32:48])
                E.drain()
                E.tensor_scalar_add(tiny[:, 32:48], tiny[:, 32:48], float(eps))
                E.drain()
                E.reciprocal(tiny[:, 32:48], tiny[:, 32:48])
                E.tensor_scalar_mul(tiny[:, 48:64], tiny[:, 0:16], -1.0)
                E.last_inc(S("c_v"), 1)
            p.on("v", v)
            s0 = p.inc("c_s", 1)
            vv = p.val("c_v")
            def s(E, vv=vv, st=st):
                E.wait_ge(S("c_v"), vv)
                E.activation(st[:, :, 0], tiny[:, 32:48], AF.Sqrt)
                E.last_inc(S("c_s"), 1)
            p.on("s", s)
            p.inc("c_v", 1)
            def v2(E, s0=s0, st=st):
                E.wait_ge(S("c_s"), s0)
                E.tensor_tensor(st[:, :, 1], tiny[:, 48:64], st[:, :, 0], ALU.mult)
                E.last_inc(S("c_v"), 1)
            p.on("v", v2)

        def ln_to_T(src_hbm, st, dstT, chbuf, appbuf, preloaded=None,
                    stats=True, eps=EPS1, dsem="d_ld", tap=None, pre_waits=(),
                    router=False):
            if stats:
                assert preloaded is not None
                xv = p.val("d_x")
                p.inc("c_v", 1)
                def v(E, xv=xv):
                    E.wait_ge(S("d_x"), xv)
                    for c in range(NCH):
                        E.reduce_sum(sac[:, c, 0:1], preloaded[:, c, :], axis=AX)
                    E.last_inc(S("c_v"), 1)
                p.on("v", v)
                s0 = p.inc("c_s", 1)
                def s(E, xv=xv):
                    E.wait_ge(S("d_x"), xv)
                    for c in range(NCH):
                        E.activation(sqd[:, c % 2, :], preloaded[:, c, :],
                                     AF.Square, accum_out=sac[:, c, 1:2])
                        E.drain()
                    E.last_inc(S("c_s"), 1)
                p.on("s", s)
                p.inc("c_v", 1)
                def vj(E, s0=s0):
                    E.wait_ge(S("c_s"), s0)
                    E.last_inc(S("c_v"), 1)
                p.on("v", vj)
                stats_batched(st, eps)
            stdone = p.val("c_v")
            ap_sig = [0] * NCH
            ev_sig = [0] * NCH
            rtr_sig = [0] * NCH
            lvals = [0] * NCH
            for c in range(NCH):
                par = c % 2
                if preloaded is None:
                    sname = dsem + str(par)
                    lvals[c] = p.inc(sname, 16)
                    def gld(E, c=c, par=par, sname=sname,
                            pw=(tuple(pre_waits) if c < 2 else ()),
                            guard=(ap_sig[c - 2] if c >= 2 else 0)):
                        for nm, vv in pw:
                            E.wait_ge(S(nm), vv)
                        if guard:
                            E.wait_ge(S("c_s"), guard)
                        E.dma_start(out=chbuf[par][:, :],
                                    in_=src_hbm.rearrange("(n p) e -> p n e", p=128)[:, c, :]
                                    ).then_inc(S(sname), 16)
                    p.on("g", gld)
                sv = p.inc("c_s", 1)
                def sf(E, c=c, par=par, stdone=stdone, lv=lvals[c],
                       guard=(ev_sig[c - 2] if c >= 2 else 0), dsem=dsem):
                    E.wait_ge(S("c_v"), stdone)
                    if preloaded is None:
                        E.wait_ge(S(dsem + str(par)), lv)
                    if guard:
                        E.wait_ge(S("c_v"), guard)
                    src = (preloaded[:, c, :] if preloaded is not None
                           else chbuf[par][:, :])
                    E.activation(appbuf[par][:, :], src, AF.Identity,
                                 bias=st[:, c, 1:2], scale=st[:, c, 0:1])
                    E.last_inc(S("c_s"), 1)
                p.on("s", sf)
                ap_sig[c] = sv
                t0 = p.inc("c_t", 1)
                def tf(E, c=c, par=par, sv=sv,
                       guard=(ev_sig[c - 2] if c >= 2 else 0)):
                    E.wait_ge(S("c_s"), sv)
                    if guard:
                        E.wait_ge(S("c_v"), guard)
                    ps = psBC[par]
                    for dc in range(DCH):
                        E.matmul(ps[:, ts(dc)], appbuf[par][:, ts(dc)],
                                 ident[:, :], start=True, stop=True,
                                 is_transpose=True)
                    E.last_inc(S("c_t"), 1)
                p.on("t", tf)
                v1 = p.inc("c_v", 1)
                def vf(E, c=c, par=par, t0=t0, router=router,
       rguard=(rtr_sig[c - 1] if (router and c >= 1) else 0)):
                    E.wait_ge(S("c_t"), t0)
                    E.tensor_copy(dstT[:, :, ts(c)],
                                  psBC[par].rearrange("p (a b) -> p a b", a=DCH))
                    if router:
                        if rguard:
                            E.wait_ge(S("c_t"), rguard)
                        E.drain()
                        E.tensor_copy(trf32[:, :, :],
                                      psBC[par].rearrange("p (a b) -> p a b", a=DCH))
                    E.last_inc(S("c_v"), 1)
                p.on("v", vf)
                ev_sig[c] = v1
                if router:
                    rt0 = p.inc("c_t", 1)
                    def trt(E, c=c, v1=v1):
                        E.wait_ge(S("c_v"), v1)
                        for dc in range(DCH):
                            E.matmul(psA[:, c * 8:c * 8 + 8], trf32[:, dc, :],
                                     rwT[:, dc, :], start=(dc == 0),
                                     stop=(dc == DCH - 1))
                        E.last_inc(S("c_t"), 1)
                    p.on("t", trt)
                    rtr_sig[c] = rt0
            if tap:
                vv = p.val("c_v")
                p.inc("d_tap", 16)
                def gt(E, vv=vv, tap=tap):
                    E.wait_ge(S("c_v"), vv)
                    E.dma_start(out=d["tap_" + tap].rearrange("(a p) m -> p a m", p=128),
                                in_=dstT[:, :, :]).then_inc(S("d_tap"), 16)
                p.on("g", gt)

        def p1():
            p.inc("d_x", 16)
            def g(E):
                E.dma_start(out=x_sb[:, :, :],
                            in_=d["x"].rearrange("(n p) e -> p n e", p=128)
                            ).then_inc(S("d_x"), 16)
            p.on("g", g)
            ln_to_T(d["x"], st0, xnTf, None, xnf, preloaded=x_sb,
                    tap=("xnT" if "xnT" in TAPS else None))

        def p2():
            pv = p.inc("d_phi", 16)
            def g(E, pv=pv):
                E.dma_start(out=phi_f[:, :, :],
                            in_=d["phi2"].rearrange("(a p) n -> p a n", p=128)
                            ).then_inc(S("d_phi"), 16)
            p.on("g", g)
            exp_sig = [0] * NCH
            stv = [0] * NCH
            for mt in range(NCH):
                par = mt % 2
                t0 = p.inc("c_t", 1)
                def t(E, mt=mt, pv=pv, guard=(exp_sig[mt - 1] if mt >= 1 else 0)):
                    if mt == 0:
                        E.wait_ge(S("d_phi"), pv)
                    if guard:
                        E.wait_ge(S("c_s"), guard)
                    for dc in range(DCH):
                        for j in range(4):
                            E.matmul(psA[:, ts(j, 512)], xnTf[:, dc, ts(mt)],
                                     phi_f[:, dc, ts(j, 512)],
                                     start=(dc == 0), stop=(dc == DCH - 1))
                    E.last_inc(S("c_t"), 1)
                p.on("t", t)
                sname = "d_E" + str(par)
                s1 = p.inc("c_s", 1)
                def sf(E, mt=mt, par=par, t0=t0, sname=sname,
                       guard=(stv[mt - 2] if mt >= 2 else 0)):
                    E.wait_ge(S("c_t"), t0)
                    if guard:
                        E.wait_ge(S(sname), guard)
                    E.activation(Ech[par][:, :], psA[:, :], AF.Exp)
                    E.last_inc(S("c_s"), 1)
                p.on("s", sf)
                exp_sig[mt] = s1
                ev = p.inc(sname, 16)
                def y(E, mt=mt, par=par, s1=s1, sname=sname):
                    E.wait_ge(S("c_s"), s1)
                    E.dma_start(out=d["E_hbm"][ts(mt), :],
                                in_=Ech[par][:, :]).then_inc(S(sname), 16)
                p.on("y", y)
                stv[mt] = ev
            e0, e1 = p.val("d_E0"), p.val("d_E1")
            p.inc("cc", 1)
            def gcc(E, e0=e0, e1=e1):
                E.wait_ge(S("d_E0"), e0)
                E.wait_ge(S("d_E1"), e1)
                E.collective_compute(
                    "AllReduce", ALU.add, replica_groups=[list(range(8))],
                    ins=[d["E_hbm"][:, :]], outs=[d["Z_hbm"][:, :]],
                ).then_inc(S("cc"), 1)
            p.on("g", gcc)
            if "E" in TAPS:
                p.inc("d_tap", 16)
                def gt(E, e0=e0, e1=e1):
                    E.wait_ge(S("d_E0"), e0)
                    E.wait_ge(S("d_E1"), e1)
                    E.dma_start(out=d["tap_E"][:, :],
                                in_=d["E_hbm"][:, :]).then_inc(S("d_tap"), 16)
                p.on("g", gt)

        def attention_l0():
            cready = p.val("d_const")
            e0, e1 = p.val("d_E0"), p.val("d_E1")
            bvv = p.inc("d_rw", 16)
            p.inc("c_g", 1)
            def g0(E, bvv=bvv, e0=e0, e1=e1):
                E.wait_ge(S("d_E0"), e0)
                E.wait_ge(S("d_E1"), e1)
                E.dma_start(out=bvf[0:1, :, :],
                            in_=d["brows"][None, 0:2, :]).then_inc(S("d_rw"), 16)
                E.memset(vsp[:, :, :, 64:66], 0.0)
                E.drain()
                E.memset(vsp[:, :, :, 64:65], 1.0)
                E.last_inc(S("c_g"), 1)
            p.on("g", g0)
            gv = p.val("c_g")

            wsig = []
            qs_sig = []
            vs_sig = []
            for pr in range(8):
                for wi, wname in enumerate(("wq", "wk", "wv")):
                    par = (pr * 3 + wi) % 2
                    sname = "d_wA" if par == 0 else "d_wB"
                    lv = p.inc(sname, 16)
                    def gw(E, wname=wname, par=par, pr=pr, sname=sname,
                           guard=(wsig[-2] if len(wsig) >= 2 else 0),
                           ew=(len(wsig) < 2), e0=e0, e1=e1):
                        if ew:
                            E.wait_ge(S("d_E0"), e0)
                            E.wait_ge(S("d_E1"), e1)
                        if guard:
                            E.wait_ge(S("c_t"), guard)
                        E.dma_start(out=wqp[par][:, :, :],
                                    in_=d[wname].rearrange("(a p) f -> p a f", p=128)[:, :, ts(pr)]
                                    ).then_inc(S(sname), 16)
                    p.on("g", gw)
                    if wi < 2:
                        col = (0 if wi == 0 else 8) + pr
                        t0 = p.inc("c_t", 1)
                        def t(E, par=par, lv=lv, sname=sname, sprev=p.val("c_s")):
                            E.wait_ge(S(sname), lv)
                            if sprev:
                                E.wait_ge(S("c_s"), sprev)
                            for dc in range(DCH):
                                for j in range(4):
                                    E.matmul(psA[:, ts(j, 512)],
                                             wqp[par][:, dc, :],
                                             xnTf[:, dc, ts(j, 512)],
                                             start=(dc == 0), stop=(dc == DCH - 1))
                            E.last_inc(S("c_t"), 1)
                        p.on("t", t)
                        wsig.append(t0)
                        s0 = p.inc("c_s", 1)
                        def sf(E, col=col, t0=t0, cready=cready,
                               guard=(qs_sig[-1] if qs_sig else 0)):
                            E.wait_ge(S("c_t"), t0)
                            E.wait_ge(S("d_const"), cready)
                            if guard:
                                E.wait_ge(S("d_qs"), guard)
                            E.activation(qkev[:, :], psA[:, :], AF.Identity,
                                         bias=bqkc[:, col:col + 1])
                            E.last_inc(S("c_s"), 1)
                        p.on("s", sf)
                        qv = p.inc("d_qs", 16)
                        def y0(E, pr=pr, wi=wi, s0=s0):
                            E.wait_ge(S("c_s"), s0)
                            E.dma_start(out=d["q_hbm" if wi == 0 else "k_hbm"][pr, :, :],
                                        in_=qkev[:, :]).then_inc(S("d_qs"), 16)
                        p.on("y", y0)
                        qs_sig.append(qv)
                    else:
                        ev_sig = [0] * NCH
                        last_t = 0
                        for mt in range(NCH):
                            t0 = p.inc("c_t", 1)
                            def t(E, mt=mt, par=par, lv=lv, sname=sname, pr=pr,
                                  gv=gv, bvv=bvv,
                                  guard=(ev_sig[mt - 2] if mt >= 2 else 0)):
                                if mt == 0:
                                    E.wait_ge(S(sname), lv)
                                    E.wait_ge(S("c_g"), gv)
                                    E.wait_ge(S("d_rw"), bvv)
                                if guard:
                                    E.wait_ge(S("c_s"), guard)
                                ps = psBC[mt % 2]
                                for dc in range(DCH):
                                    E.matmul(ps[:, 0:128], xnTf[:, dc, ts(mt)],
                                             wqp[par][:, dc, :],
                                             start=(dc == 0), stop=False)
                                E.matmul(ps[:, 0:128], onesf[0:1, 0:128],
                                         bvf[0:1, 0, ts(pr)], start=False, stop=True)
                                E.last_inc(S("c_t"), 1)
                            p.on("t", t)
                            last_t = p.val("c_t")
                            s1 = p.inc("c_s", 1)
                            def sv(E, mt=mt, t0=t0,
                                   guard=(vs_sig[-1] if (mt == 0 and vs_sig) else 0)):
                                E.wait_ge(S("c_t"), t0)
                                if guard:
                                    E.wait_ge(S("d_vs"), guard)
                                E.activation(vsp[:, mt, 0:2, 0:64],
                                             psBC[mt % 2][:, 0:128], AF.Identity)
                                E.last_inc(S("c_s"), 1)
                            p.on("s", sv)
                            ev_sig[mt] = s1
                        wsig.append(last_t)
                        vv = p.inc("d_vs", 16)
                        sall = p.val("c_s")
                        def y1(E, pr=pr, sall=sall):
                            E.wait_ge(S("c_s"), sall)
                            E.dma_start(out=d["v_hbm"][pr, :, :],
                                        in_=vsp.rearrange("p a b c -> p (a b c)")
                                        ).then_inc(S("d_vs"), 16)
                        p.on("y", y1)
                        vs_sig.append(vv)
            qsv, vsv = p.val("d_qs"), p.val("d_vs")

            p.inc("d_castA", 27 * 16)
            def gcast(E):
                for e in range(9):
                    if e == 0:
                        sg_, su_, sd_ = d["swg"][:, :], d["swu"][:, :], d["swd"][:, :]
                    else:
                        sg_, su_, sd_ = (d["ewg"][e - 1, :, :], d["ewu"][e - 1, :, :],
                                         d["ewd"][e - 1, :, :])
                    E.dma_start(out=d["wg_bf9"][e, :, :], in_=sg_).then_inc(S("d_castA"), 16)
                    E.dma_start(out=d["wu_bf9"][e, :, :], in_=su_).then_inc(S("d_castA"), 16)
                    E.dma_start(out=d["wd_bf9"][e, :, :], in_=sd_).then_inc(S("d_castA"), 16)
            p.on("g", gcast)

            prev_ve = 0
            outproj_done = 0
            st_sig = [0] * NCH
            gbv = 0
            for qh in range(2):
                ld_sig = []
                for pr in range(8):
                    par = pr % 2
                    sname = "d_ld" + str(par)
                    lv = p.inc(sname, 48)
                    def gl(E, pr=pr, qh=qh, par=par, sname=sname, qsv=qsv, vsv=vsv,
                           guard=(ld_sig[-2] if len(ld_sig) >= 2 else 0),
                           opd=(outproj_done if (qh == 1 and pr < 2) else 0)):
                        E.wait_ge(S("d_qs"), qsv)
                        E.wait_ge(S("d_vs"), vsv)
                        if opd:
                            E.wait_ge(S("c_t"), opd)
                        if guard:
                            E.wait_ge(S("c_t"), guard)
                        E.dma_start(out=qTh[par][:, :],
                                    in_=d["q_hbm"][pr, :, ts(qh, 1024)]).then_inc(S(sname), 16)
                        E.dma_start(out=kTp[par][:, :],
                                    in_=d["k_hbm"][pr, :, :]).then_inc(S(sname), 16)
                        E.dma_start(out=vp[par].rearrange("p a b c -> p (a b c)"),
                                    in_=d["v_hbm"][pr, :, :]).then_inc(S(sname), 16)
                    p.on("g", gl)
                    exp_sig = [0] * NCH
                    last_ctx = 0
                    for scn in range(NCH):
                        apar = scn % 2
                        t0 = p.inc("c_t", 1)
                        def t(E, scn=scn, par=par, lv=lv, sname=sname,
                              guard=(exp_sig[scn - 1] if scn >= 1 else 0),
                              pve=prev_ve):
                            if scn == 0:
                                E.wait_ge(S(sname), lv)
                                if pve:
                                    E.wait_ge(S("c_v"), pve)
                            if guard:
                                E.wait_ge(S("c_s"), guard)
                            for j in range(2):
                                E.matmul(psA[:, ts(j, 512)],
                                         kTp[par][0:64, ts(scn)],
                                         qTh[par][0:64, ts(j, 512)],
                                         start=True, stop=True, tile_position=(0, 0))
                                E.matmul(psA[:, 1024 + j * 512:1536 + j * 512],
                                         kTp[par][64:128, ts(scn)],
                                         qTh[par][64:128, ts(j, 512)],
                                         start=True, stop=True, tile_position=(64, 0))
                            E.last_inc(S("c_t"), 1)
                        p.on("t", t)
                        s1 = p.inc("c_s", 1)
                        def sf(E, apar=apar, t0=t0):
                            E.wait_ge(S("c_t"), t0)
                            E.activation(attnT0[apar][:, :], psA[:, :],
                                         AF.Exp, scale=0.125)
                            E.last_inc(S("c_s"), 1)
                        p.on("s", sf)
                        exp_sig[scn] = s1
                        t1 = p.inc("c_t", 1)
                        def tc(E, scn=scn, apar=apar, par=par, s1=s1, pve=prev_ve):
                            E.wait_ge(S("c_s"), s1)
                            if scn == 0 and pve:
                                E.wait_ge(S("c_v"), pve)
                            first, last = scn == 0, scn == NCH - 1
                            for hh in range(2):
                                for j in range(2):
                                    E.matmul(psBC[hh][0:65, ts(j, 512)],
                                             vp[par][:, scn, hh, 0:65],
                                             attnT0[apar][:, hh * 1024 + j * 512:
                                                          hh * 1024 + (j + 1) * 512],
                                             start=first, stop=last)
                            E.last_inc(S("c_t"), 1)
                        p.on("t", tc)
                        last_ctx = t1
                    ld_sig.append(last_ctx)
                    tctx = p.val("c_t")
                    v0 = p.inc("c_v", 1)
                    def vd(E, tctx=tctx):
                        E.wait_ge(S("c_t"), tctx)
                        E.tensor_copy(rowsc[0:1, 0:2, :],
                                      psB[64:65, :].rearrange("p (a b) -> p a b", a=2))
                        E.tensor_copy(rowsc[0:1, 2:4, :],
                                      psC[64:65, :].rearrange("p (a b) -> p a b", a=2))
                        E.drain()
                        E.reciprocal(rowsc[0:1, :, :], rowsc[0:1, :, :])
                        E.last_inc(S("c_v"), 1)
                    p.on("v", vd)
                    t2 = p.inc("c_t", 1)
                    def tb(E, v0=v0):
                        E.wait_ge(S("c_v"), v0)
                        for j in range(4):
                            E.matmul(psA[0:64, ts(j, 512)], onesf[0:1, 0:64],
                                     rowsc[0:1, j, :], start=True, stop=True)
                        E.last_inc(S("c_t"), 1)
                    p.on("t", tb)
                    v1 = p.inc("c_v", 1)
                    def ve(E, pr=pr, t2=t2):
                        E.wait_ge(S("c_t"), t2)
                        E.tensor_copy(bcf[0:64, :], psA[0:64, :])
                        E.drain()
                        for hh in range(2):
                            hg = 2 * pr + hh
                            for j in range(2):
                                E.tensor_tensor(
                                    ctxTh[(hg % 2) * 64:(hg % 2) * 64 + 64,
                                          hg // 2, ts(j, 512)],
                                    psBC[hh][0:64, ts(j, 512)],
                                    bcf[0:64, ts(2 * hh + j, 512)],
                                    ALU.mult)
                        E.last_inc(S("c_v"), 1)
                    p.on("v", ve)
                    prev_ve = v1
                core_done = p.val("c_t")
                wov = p.inc("d_wA", 16)
                if qh == 0:
                    gbv = p.inc("d_gb", 32)
                def gwo(E, cd=core_done, qh=qh):
                    E.wait_ge(S("c_t"), cd)
                    E.dma_start(out=wof[:, :, :],
                                in_=d["wo"].rearrange("(a p) f -> p a f", p=128)
                                ).then_inc(S("d_wA"), 16)
                    if qh == 0:
                        E.dma_start(out=gb_sb[:, 0:1, :],
                                    in_=d["gamma_bc"][:, None, :]).then_inc(S("d_gb"), 16)
                        E.dma_start(out=gb_sb[:, 1:2, :],
                                    in_=d["beta_bc"][:, None, :]).then_inc(S("d_gb"), 16)
                p.on("g", gwo)
                ln_sig = [0] * 8
                pv_sig = [0] * 8
                ps_free = [0] * 8
                s4_sig = [0] * 8
                v3_sig = [0] * 8
                for i in range(8):
                    mt = qh * 8 + i
                    par = i % 2
                    sname = "d_lx" + str(par)
                    lx = p.inc(sname, 16)
                    def gx(E, mt=mt, sname=sname,
                           guard=(ln_sig[i - 1] if i >= 1 else 0)):
                        if guard:
                            E.wait_ge(S("c_s"), guard)
                        E.dma_start(out=xch3[:, :],
                                    in_=d["x"].rearrange("(n p) e -> p n e", p=128)[:, mt, :]
                                    ).then_inc(S(sname), 16)
                    p.on("g", gx)
                    t0 = p.inc("c_t", 1)
                    def t(E, i=i, par=par, wov=wov, pve=prev_ve,
                          guard=(ps_free[i - 2] if i >= 2 else 0)):
                        if i == 0:
                            E.wait_ge(S("d_wA"), wov)
                            E.wait_ge(S("c_v"), pve)
                        if guard:
                            E.wait_ge(S("c_v"), guard)
                        ps = psBC[par]
                        for hc in range(DCH):
                            for j in range(2):
                                E.matmul(ps[:, ts(j, 512)], ctxTh[:, hc, ts(i)],
                                         wof[:, hc, ts(j, 512)],
                                         start=(hc == 0), stop=False)
                        for j in range(2):
                            E.matmul(ps[:, ts(j, 512)], onesf[0:1, 0:128],
                                     bvf[0:1, 1, ts(j, 512)], start=False, stop=True)
                        E.last_inc(S("c_t"), 1)
                    p.on("t", t)
                    s1 = p.inc("c_s", 1)
                    def sf(E, mt=mt, lx=lx, sname=sname,
                           guard=(pv_sig[i - 1] if i >= 1 else 0)):
                        E.wait_ge(S(sname), lx)
                        if guard:
                            E.wait_ge(S("c_v"), guard)
                        E.activation(xnch3[:, :], xch3[:, :], AF.Identity,
                                     bias=st0[:, mt, 1:2], scale=st0[:, mt, 0:1])
                        E.last_inc(S("c_s"), 1)
                    p.on("s", sf)
                    ln_sig[i] = s1
                    v0 = p.inc("c_v", 1)
                    def vf(E, par=par, s1=s1, t0=t0,
                           guard=(s4_sig[i - 1] if i >= 1 else 0)):
                        E.wait_ge(S("c_s"), s1)
                        E.wait_ge(S("c_t"), t0)
                        if guard:
                            E.wait_ge(S("c_s"), guard)
                        E.tensor_add(ych3[:, :], xnch3[:, :], psBC[par][:, :])
                        E.drain()
                        E.reduce_sum(tiny[:, 0:1], ych3[:, :], axis=AX)
                        E.last_inc(S("c_v"), 1)
                    p.on("v", vf)
                    pv_sig[i] = v0
                    ps_free[i] = v0
                    s2 = p.inc("c_s", 1)
                    def sf2(E, v0=v0):
                        E.wait_ge(S("c_v"), v0)
                        E.activation(sqd[:, 0, :], ych3[:, :], AF.Square,
                                     accum_out=tiny[:, 1:2])
                        E.last_inc(S("c_s"), 1)
                    p.on("s", sf2)
                    v1 = p.inc("c_v", 1)
                    def vf2(E, s2=s2):
                        E.wait_ge(S("c_s"), s2)
                        E.tensor_scalar_mul(tiny[:, 2:3], tiny[:, 0:1], 1.0 / D)
                        E.tensor_scalar_mul(tiny[:, 3:4], tiny[:, 1:2], 1.0 / D)
                        E.drain()
                        E.tensor_tensor(tiny[:, 4:5], tiny[:, 2:3], tiny[:, 2:3],
                                        ALU.mult)
                        E.drain()
                        E.tensor_sub(tiny[:, 4:5], tiny[:, 3:4], tiny[:, 4:5])
                        E.drain()
                        E.tensor_scalar_add(tiny[:, 4:5], tiny[:, 4:5], EPS2)
                        E.drain()
                        E.reciprocal(tiny[:, 4:5], tiny[:, 4:5])
                        E.tensor_scalar_mul(tiny[:, 5:6], tiny[:, 2:3], -1.0)
                        E.last_inc(S("c_v"), 1)
                    p.on("v", vf2)
                    s3 = p.inc("c_s", 1)
                    def sf3(E, v1=v1):
                        E.wait_ge(S("c_v"), v1)
                        E.activation(tiny[:, 6:7], tiny[:, 4:5], AF.Sqrt)
                        E.last_inc(S("c_s"), 1)
                    p.on("s", sf3)
                    v2 = p.inc("c_v", 1)
                    def vf3(E, s3=s3):
                        E.wait_ge(S("c_s"), s3)
                        E.tensor_tensor(tiny[:, 7:8], tiny[:, 5:6], tiny[:, 6:7],
                                        ALU.mult)
                        E.last_inc(S("c_v"), 1)
                    p.on("v", vf3)
                    s4 = p.inc("c_s", 1)
                    def sf4(E, v2=v2,
                            guard=(v3_sig[i - 1] if i >= 1 else 0)):
                        E.wait_ge(S("c_v"), v2)
                        if guard:
                            E.wait_ge(S("c_v"), guard)
                        E.activation(zch3[:, :], ych3[:, :], AF.Identity,
                                     bias=tiny[:, 7:8], scale=tiny[:, 6:7])
                        E.last_inc(S("c_s"), 1)
                    p.on("s", sf4)
                    s4_sig[i] = s4
                    v3 = p.inc("c_v", 1)
                    def vf4(E, par=par, s4=s4, gbv=gbv,
                            guard=(st_sig[mt - 2] if mt >= 2 else 0)):
                        E.wait_ge(S("c_s"), s4)
                        E.wait_ge(S("d_gb"), gbv)
                        if guard:
                            E.wait_ge(S("d_s" + str(par)), guard)
                        E.tensor_tensor(sch3[par][:, :], zch3[:, :],
                                        gb_sb[:, 0, :], ALU.mult)
                        E.drain()
                        E.tensor_add(sch3[par][:, :], sch3[par][:, :], gb_sb[:, 1, :])
                        E.last_inc(S("c_v"), 1)
                    p.on("v", vf4)
                    v3_sig[i] = v3
                    svv = p.inc("d_s" + str(par), 16)
                    def gst(E, mt=mt, par=par, v3=v3):
                        E.wait_ge(S("c_v"), v3)
                        E.dma_start(out=d["s_hbm"][ts(mt), :],
                                    in_=sch3[par][:, :]).then_inc(S("d_s" + str(par)), 16)
                    p.on("y", gst)
                    st_sig[mt] = svv
                outproj_done = p.val("c_t")
            if "s" in TAPS:
                a, b = p.val("d_s0"), p.val("d_s1")
                p.inc("d_tap", 16)
                def gt(E, a=a, b=b):
                    E.wait_ge(S("d_s0"), a)
                    E.wait_ge(S("d_s1"), b)
                    E.dma_start(out=d["tap_s"][:, :],
                                in_=d["s_hbm"][:, :]).then_inc(S("d_tap"), 16)
                p.on("g", gt)

        def attention(lay):
            wqn, wkn, wvn, won = (("wq", "wk", "wv", "wo") if lay == 0
                                  else ("wq1", "wk1", "wv1", "wout"))
            qb, kb = (0, 8) if lay == 0 else (16, 24)
            vrow, orow = 0, 1
            s_spill = d["s_hbm"] if lay == 0 else d["s1_hbm"]
            ssem = "d_s" if lay == 0 else "d_s1"

            p.inc("c_g", 1)
            def gi(E):
                E.memset(v_att[:, :, :, 64:66], 0.0)
                E.drain()
                E.memset(v_att[:, :, :, 64:65], 1.0)
                E.last_inc(S("c_g"), 1)
            p.on("g", gi)
            gready = p.val("c_g")
            cready = p.val("d_const")

            prev_ve = 0
            for grp in range(2):
                grp_tguard = p.val("c_t")
                wav = p.inc("d_wA", 32)
                def gw(E, grp=grp, gg=grp_tguard):
                    if gg:
                        E.wait_ge(S("c_t"), gg)
                    E.dma_start(out=wA[:, :, 0:512],
                                in_=d[wqn].rearrange("(a p) f -> p a f", p=128)[:, :, ts(grp, 512)]
                                ).then_inc(S("d_wA"), 16)
                    E.dma_start(out=wA[:, :, 512:1024],
                                in_=d[wkn].rearrange("(a p) f -> p a f", p=128)[:, :, ts(grp, 512)]
                                ).then_inc(S("d_wA"), 16)
                p.on("g", gw)
                wbv = p.inc("d_wB", 16)
                def gw2(E, grp=grp, gg=grp_tguard):
                    if gg:
                        E.wait_ge(S("c_t"), gg)
                    E.dma_start(out=wB[:, :, 0:512],
                                in_=d[wvn].rearrange("(a p) f -> p a f", p=128)[:, :, ts(grp, 512)]
                                ).then_inc(S("d_wB"), 16)
                p.on("g", gw2)

                for which in ("q", "k"):
                    dst = qTg if which == "q" else kTg
                    colbase = qb if which == "q" else kb
                    off = 0 if which == "q" else 512
                    for ht in range(4):
                        pe = p.val("c_s")
                        t0 = p.inc("c_t", 1)
                        def t(E, ht=ht, off=off, wav=wav, pe=pe,
                              first=(which == "q" and ht == 0),
                              pve=(prev_ve if (which == "q" and ht == 0) else 0)):
                            if first:
                                E.wait_ge(S("d_wA"), wav)
                            if pve:
                                E.wait_ge(S("c_v"), pve)
                            if pe:
                                E.wait_ge(S("c_s"), pe)
                            for dc in range(DCH):
                                for j in range(4):
                                    E.matmul(psA[:, ts(j, 512)],
                                             wA[:, dc, off + ht * 128:off + (ht + 1) * 128],
                                             xnT[:, dc, ts(j, 512)],
                                             start=(dc == 0), stop=(dc == DCH - 1))
                            E.last_inc(S("c_t"), 1)
                        p.on("t", t)
                        col = colbase + grp * 4 + ht
                        p.inc("c_s", 1)
                        def sf(E, ht=ht, dst=dst, col=col, t0=t0,
                               cready=(cready if (which == "q" and ht == 0 and grp == 0) else 0)):
                            E.wait_ge(S("c_t"), t0)
                            if cready:
                                E.wait_ge(S("d_const"), cready)
                            E.activation(dst[:, ht, :], psA[:, :], AF.Identity,
                                         bias=bqkc[:, col:col + 1])
                            E.last_inc(S("c_s"), 1)
                        p.on("s", sf)

                vev = [0] * NCH
                for mt in range(NCH):
                    par = mt % 2
                    t0 = p.inc("c_t", 1)
                    def t(E, mt=mt, par=par, wbv=wbv, grp=grp,
                          guard=(vev[mt - 2] if mt >= 2 else 0),
                          gready=gready, cready=cready):
                        if mt == 0:
                            E.wait_ge(S("d_wB"), wbv)
                            E.wait_ge(S("c_g"), gready)
                            E.wait_ge(S("d_const"), cready)
                        if guard:
                            E.wait_ge(S("c_s"), guard)
                        ps = psBC[par]
                        for dc in range(DCH):
                            E.matmul(ps[:, 0:512], xnT[:, dc, ts(mt)],
                                     wB[:, dc, 0:512], start=(dc == 0), stop=False)
                        E.matmul(ps[:, 0:512], ones[0:1, 0:128],
                                 brow[0:1, vrow, ts(grp, 512)],
                                 start=False, stop=True)
                        E.last_inc(S("c_t"), 1)
                    p.on("t", t)
                    s1 = p.inc("c_s", 1)
                    def sf(E, mt=mt, par=par, grp=grp, t0=t0):
                        E.wait_ge(S("c_t"), t0)
                        E.activation(v_att[:, mt, grp * 8:(grp + 1) * 8, 0:64],
                                     psBC[par][:, 0:512], AF.Identity)
                        E.last_inc(S("c_s"), 1)
                    p.on("s", sf)
                    vev[mt] = s1

                for pr in range(4):
                    he = grp * 8 + 2 * pr
                    for qh in range(2):
                        exp_sig = [0] * NCH
                        for scn in range(NCH):
                            par = scn % 2
                            t0 = p.inc("c_t", 1)
                            def t(E, scn=scn, pr=pr, qh=qh,
                                  guard=(exp_sig[scn - 1] if scn >= 1 else 0),
                                  pve=prev_ve):
                                if scn == 0 and pve:
                                    E.wait_ge(S("c_v"), pve)
                                if guard:
                                    E.wait_ge(S("c_s"), guard)
                                for j in range(2):
                                    q0 = qh * 1024 + j * 512
                                    E.matmul(psA[:, ts(j, 512)],
                                             kTg[0:64, pr, ts(scn)],
                                             qTg[0:64, pr, q0:q0 + 512],
                                             start=True, stop=True,
                                             tile_position=(0, 0))
                                    E.matmul(psA[:, 1024 + j * 512:1536 + j * 512],
                                             kTg[64:128, pr, ts(scn)],
                                             qTg[64:128, pr, q0:q0 + 512],
                                             start=True, stop=True,
                                             tile_position=(64, 0))
                                E.last_inc(S("c_t"), 1)
                            p.on("t", t)
                            s1 = p.inc("c_s", 1)
                            def sf(E, par=par, t0=t0):
                                E.wait_ge(S("c_t"), t0)
                                E.activation(attnT[par][:, :], psA[:, :],
                                             AF.Exp, scale=0.125)
                                E.last_inc(S("c_s"), 1)
                            p.on("s", sf)
                            exp_sig[scn] = s1
                            p.inc("c_t", 1)
                            def tc(E, scn=scn, par=par, he=he, s1=s1, pve=prev_ve):
                                E.wait_ge(S("c_s"), s1)
                                if scn == 0 and pve:
                                    E.wait_ge(S("c_v"), pve)
                                first, last = scn == 0, scn == NCH - 1
                                for hh in range(2):
                                    for j in range(2):
                                        E.matmul(psBC[hh][0:65, ts(j, 512)],
                                                 v_att[:, scn, he + hh, 0:65],
                                                 attnT[par][:, hh * 1024 + j * 512:
                                                            hh * 1024 + (j + 1) * 512],
                                                 start=first, stop=last)
                                E.last_inc(S("c_t"), 1)
                            p.on("t", tc)
                        tctx = p.val("c_t")
                        v0 = p.inc("c_v", 1)
                        def vd(E, tctx=tctx):
                            E.wait_ge(S("c_t"), tctx)
                            E.tensor_copy(rowsc[0:1, 0:2, :],
                                          psB[64:65, :].rearrange("p (a b) -> p a b", a=2))
                            E.tensor_copy(rowsc[0:1, 2:4, :],
                                          psC[64:65, :].rearrange("p (a b) -> p a b", a=2))
                            E.drain()
                            E.reciprocal(rowsc[0:1, :, :], rowsc[0:1, :, :])
                            E.last_inc(S("c_v"), 1)
                        p.on("v", vd)
                        t2 = p.inc("c_t", 1)
                        def tb(E, v0=v0):
                            E.wait_ge(S("c_v"), v0)
                            for j in range(4):
                                E.matmul(psA[0:64, ts(j, 512)], onesf[0:1, 0:64],
                                         rowsc[0:1, j, :], start=True, stop=True)
                            E.last_inc(S("c_t"), 1)
                        p.on("t", tb)
                        v1 = p.inc("c_v", 1)
                        def ve(E, qh=qh, he=he, t2=t2):
                            E.wait_ge(S("c_t"), t2)
                            E.tensor_copy(bcf1[0:64, :], psA[0:64, :])
                            E.drain()
                            for hh in range(2):
                                hg = he + hh
                                for j in range(2):
                                    q0 = qh * 1024 + j * 512
                                    E.tensor_tensor(
                                        ctxT[(hg % 2) * 64:(hg % 2) * 64 + 64,
                                             hg // 2, q0:q0 + 512],
                                        psBC[hh][0:64, ts(j, 512)],
                                        bcf1[0:64, ts(2 * hh + j, 512)],
                                        ALU.mult)
                            E.last_inc(S("c_v"), 1)
                        p.on("v", ve)
                        prev_ve = v1

            t_all = p.val("c_t")
            wov = p.inc("d_wA", 16)
            def gwo(E, wov=wov, tg=t_all):
                E.wait_ge(S("c_t"), tg)
                E.dma_start(out=wA[:, :, :],
                            in_=d[won].rearrange("(a p) f -> p a f", p=128)
                            ).then_inc(S("d_wA"), 16)
            p.on("g", gwo)
            ps_free = [0] * NCH
            buf_free = [0] * NCH
            st_sig = [0] * NCH
            lvs = [0] * NCH
            for mt in range(NCH):
                par = mt % 2
                if lay == 0:
                    sname = "d_ld" + str(par)
                    lvs[mt] = p.inc(sname, 16)
                    def gx(E, mt=mt, par=par, sname=sname,
                           guard=(buf_free[mt - 2] if mt >= 2 else 0)):
                        if guard:
                            E.wait_ge(S("c_s"), guard)
                        E.dma_start(out=xch[par][:, :],
                                    in_=d["x"].rearrange("(n p) e -> p n e", p=128)[:, mt, :]
                                    ).then_inc(S(sname), 16)
                    p.on("g", gx)
                t0 = p.inc("c_t", 1)
                def t(E, mt=mt, wov=wov, pve=prev_ve, orow=orow, lay=lay,
                      guard=(ps_free[mt - 2] if mt >= 2 else 0)):
                    if mt == 0:
                        E.wait_ge(S("d_wA"), wov)
                        E.wait_ge(S("c_v"), pve)
                    if guard:
                        E.wait_ge(S("c_v" if lay == 0 else "c_s"), guard)
                    ps = psBC[mt % 2]
                    for hc in range(DCH):
                        for j in range(2):
                            E.matmul(ps[:, ts(j, 512)], ctxT[:, hc, ts(mt)],
                                     wA[:, hc, ts(j, 512)],
                                     start=(hc == 0), stop=False)
                    for j in range(2):
                        E.matmul(ps[:, ts(j, 512)], ones[0:1, 0:128],
                                 brow[0:1, orow, ts(j, 512)],
                                 start=False, stop=True)
                    E.last_inc(S("c_t"), 1)
                p.on("t", t)
                if lay == 0:
                    sname = "d_ld" + str(par)
                    s1 = p.inc("c_s", 1)
                    def sf(E, mt=mt, par=par, lv=lvs[mt], sname=sname,
                           guard=(ps_free[mt - 2] if mt >= 2 else 0)):
                        E.wait_ge(S(sname), lv)
                        if guard:
                            E.wait_ge(S("c_v"), guard)
                        E.activation(xnf2[par][:, :], xch[par][:, :], AF.Identity,
                                     bias=st0[:, mt, 1:2], scale=st0[:, mt, 0:1])
                        E.last_inc(S("c_s"), 1)
                    p.on("s", sf)
                    v0 = p.inc("c_v", 1)
                    def vf(E, mt=mt, par=par, s1=s1, t0=t0):
                        E.wait_ge(S("c_s"), s1)
                        E.wait_ge(S("c_t"), t0)
                        E.tensor_add(ych[par][:, :], xnf2[par][:, :], psBC[par][:, :])
                        E.drain()
                        E.reduce_sum(tiny[:, 8 * par:8 * par + 1], ych[par][:, :], axis=AX)
                        E.last_inc(S("c_v"), 1)
                    p.on("v", vf)
                    ps_free[mt] = v0
                    s2 = p.inc("c_s", 1)
                    def sf2(E, par=par, v0=v0):
                        E.wait_ge(S("c_v"), v0)
                        E.activation(sqd[:, par, :], ych[par][:, :], AF.Square,
                                     accum_out=tiny[:, 8 * par + 1:8 * par + 2])
                        E.last_inc(S("c_s"), 1)
                    p.on("s", sf2)
                    v1 = p.inc("c_v", 1)
                    def vf2(E, par=par, s2=s2):
                        b = 8 * par
                        E.wait_ge(S("c_s"), s2)
                        E.tensor_scalar_mul(tiny[:, b + 2:b + 3], tiny[:, b:b + 1], 1.0 / D)
                        E.tensor_scalar_mul(tiny[:, b + 3:b + 4], tiny[:, b + 1:b + 2], 1.0 / D)
                        E.drain()
                        E.tensor_tensor(tiny[:, b + 4:b + 5], tiny[:, b + 2:b + 3],
                                        tiny[:, b + 2:b + 3], ALU.mult)
                        E.drain()
                        E.tensor_sub(tiny[:, b + 4:b + 5], tiny[:, b + 3:b + 4],
                                     tiny[:, b + 4:b + 5])
                        E.drain()
                        E.tensor_scalar_add(tiny[:, b + 4:b + 5], tiny[:, b + 4:b + 5], EPS2)
                        E.drain()
                        E.reciprocal(tiny[:, b + 4:b + 5], tiny[:, b + 4:b + 5])
                        E.tensor_scalar_mul(tiny[:, b + 5:b + 6], tiny[:, b + 2:b + 3], -1.0)
                        E.last_inc(S("c_v"), 1)
                    p.on("v", vf2)
                    s3 = p.inc("c_s", 1)
                    def sf3(E, par=par, v1=v1):
                        b = 8 * par
                        E.wait_ge(S("c_v"), v1)
                        E.activation(tiny[:, b + 6:b + 7], tiny[:, b + 4:b + 5], AF.Sqrt)
                        E.last_inc(S("c_s"), 1)
                    p.on("s", sf3)
                    v2 = p.inc("c_v", 1)
                    def vf3(E, par=par, s3=s3):
                        b = 8 * par
                        E.wait_ge(S("c_s"), s3)
                        E.tensor_tensor(tiny[:, b + 7:b + 8], tiny[:, b + 5:b + 6],
                                        tiny[:, b + 6:b + 7], ALU.mult)
                        E.last_inc(S("c_v"), 1)
                    p.on("v", vf3)
                    s4 = p.inc("c_s", 1)
                    def sf4(E, par=par, v2=v2):
                        b = 8 * par
                        E.wait_ge(S("c_v"), v2)
                        E.activation(zch[par][:, :], ych[par][:, :], AF.Identity,
                                     bias=tiny[:, b + 7:b + 8], scale=tiny[:, b + 6:b + 7])
                        E.last_inc(S("c_s"), 1)
                    p.on("s", sf4)
                    buf_free[mt] = s4
                    v3 = p.inc("c_v", 1)
                    def vf4(E, mt=mt, par=par, s4=s4, ssem=ssem,
                            guard=(st_sig[mt - 2] if mt >= 2 else 0),
                            cready=(cready if mt == 0 else 0)):
                        E.wait_ge(S("c_s"), s4)
                        if cready:
                            E.wait_ge(S("d_const"), cready)
                        if guard:
                            E.wait_ge(S(ssem + str(par)), guard)
                        E.tensor_tensor(sch[par][:, :], zch[par][:, :],
                                        gb_sb[:, 0, :], ALU.mult)
                        E.drain()
                        E.tensor_add(sch[par][:, :], sch[par][:, :], gb_sb[:, 1, :])
                        E.last_inc(S("c_v"), 1)
                    p.on("v", vf4)
                    sv = p.inc(ssem + str(par), 16)
                    def gst(E, mt=mt, par=par, v3=v3, ssem=ssem):
                        E.wait_ge(S("c_v"), v3)
                        E.dma_start(out=s_spill[ts(mt), :],
                                    in_=sch[par][:, :]).then_inc(S(ssem + str(par)), 16)
                    p.on("y", gst)
                    st_sig[mt] = sv
                else:
                    s1 = p.inc("c_s", 1)
                    def sf(E, mt=mt, par=par, t0=t0, ssem=ssem,
                           guard=(st_sig[mt - 2] if mt >= 2 else 0)):
                        E.wait_ge(S("c_t"), t0)
                        if guard:
                            E.wait_ge(S(ssem + str(par)), guard)
                        E.activation(sch[par][:, :], psBC[par][:, :], AF.Identity)
                        E.last_inc(S("c_s"), 1)
                    p.on("s", sf)
                    ps_free[mt] = s1
                    sv = p.inc(ssem + str(par), 16)
                    def gst(E, mt=mt, par=par, s1=s1, ssem=ssem):
                        E.wait_ge(S("c_s"), s1)
                        E.dma_start(out=s_spill[ts(mt), :],
                                    in_=sch[par][:, :]).then_inc(S(ssem + str(par)), 16)
                    p.on("y", gst)
                    st_sig[mt] = sv
            if ("s" in TAPS and lay == 0) or ("s1" in TAPS and lay == 1):
                a, b = p.val(ssem + "0"), p.val(ssem + "1")
                p.inc("d_tap", 16)
                def gt(E, a=a, b=b, lay=lay, ssem=ssem):
                    E.wait_ge(S(ssem + "0"), a)
                    E.wait_ge(S(ssem + "1"), b)
                    E.dma_start(out=d["tap_s" if lay == 0 else "tap_s1"][:, :],
                                in_=s_spill[:, :]).then_inc(S("d_tap"), 16)
                p.on("g", gt)

        def p4():
            ccv = p.val("cc")
            ap_sig = [0] * NCH
            for c in range(NCH):
                par = c % 2
                sname = "d_ld" + str(par)
                lv = p.inc(sname, 16)
                def gld(E, c=c, par=par, sname=sname,
                        sw=((p.val("d_s0"), p.val("d_s1")) if c < 2 else None),
                        guard=(ap_sig[c - 2] if c >= 2 else 0)):
                    if sw:
                        E.wait_ge(S("d_s0"), sw[0])
                        E.wait_ge(S("d_s1"), sw[1])
                    if guard:
                        E.wait_ge(S("c_s"), guard)
                    E.dma_start(out=Ezb[par][:, :],
                                in_=d["x"].rearrange("(n p) e -> p n e", p=128)[:, c, :]
                                ).then_inc(S(sname), 16)
                p.on("g", gld)
                s1 = p.inc("c_s", 1)
                def sf(E, c=c, par=par, lv=lv, sname=sname):
                    E.wait_ge(S(sname), lv)
                    E.activation(xn_tok[:, c, :], Ezb[par][:, :], AF.Identity,
                                 bias=st0[:, c, 1:2], scale=st0[:, c, 0:1])
                    E.last_inc(S("c_s"), 1)
                p.on("s", sf)
                ap_sig[c] = s1
            xn_tok_done = p.val("c_s")

            dv_sig = [0] * (2 * NCH)
            half_done = [0, 0]
            sev_st = [0] * (2 * DCH)
            slot_ev = [0] * (2 * DCH)
            for nh in range(2):
                for c in range(NCH):
                    gi_ = nh * NCH + c
                    par = c % 2
                    sname = "d_ez" + str(par)
                    lv = p.inc(sname, 32)
                    def gld(E, c=c, nh=nh, par=par, sname=sname, ccv=ccv,
                            xd=(xn_tok_done if (nh == 0 and c < 2) else 0),
                            guard=(dv_sig[gi_ - 2] if gi_ >= 2 else 0)):
                        if c == 0 and nh == 0:
                            E.wait_ge(S("cc"), ccv)
                        if xd:
                            E.wait_ge(S("c_s"), xd)
                        if guard:
                            E.wait_ge(S("c_v"), guard)
                        E.dma_start(out=Ezb[par][:, :],
                                    in_=d["E_hbm"][ts(c), ts(nh, 1024)]
                                    ).then_inc(S(sname), 16)
                        E.dma_start(out=Zzb[par][:, :],
                                    in_=d["Z_hbm"][ts(c), ts(nh, 1024)]
                                    ).then_inc(S(sname), 16)
                    p.on("g", gld)
                    v0 = p.inc("c_v", 1)
                    def vf(E, c=c, par=par, lv=lv, sname=sname,
                           hw=(half_done[0] if (nh == 1 and c < 2) else 0),
                           sw=((p.val("d_s0"), p.val("d_s1"))
                               if (c == 0 and nh == 0) else None)):
                        if sw:
                            E.wait_ge(S("d_s0"), sw[0])
                            E.wait_ge(S("d_s1"), sw[1])
                        if hw:
                            E.wait_ge(S("c_s"), hw)
                        E.wait_ge(S(sname), lv)
                        E.reciprocal(Zzb[par][:, :], Zzb[par][:, :])
                        E.drain()
                        E.tensor_tensor(disp[:, c, :], Ezb[par][:, :],
                                        Zzb[par][:, :], ALU.mult)
                        E.last_inc(S("c_v"), 1)
                    p.on("v", vf)
                    dv_sig[gi_] = v0
                disp_done = p.val("c_v")
                for dc in range(DCH):
                    gi2 = nh * DCH + dc
                    pssel = gi2 % 2
                    prevps = slot_ev[gi2 - 2] if gi2 >= 2 else 0
                    t0 = p.inc("c_t", 1)
                    def tf(E, dc=dc, pssel=pssel, dd=disp_done, prevps=prevps,
                           xd=xn_tok_done):
                        E.wait_ge(S("c_v"), dd)
                        E.wait_ge(S("c_s"), xd)
                        if prevps:
                            E.wait_ge(S("c_s"), prevps)
                        ps = psBC[pssel]
                        for c in range(NCH):
                            for j in range(2):
                                E.matmul(ps[:, ts(j, 512)], xn_tok[:, c, ts(dc)],
                                         disp[:, c, ts(j, 512)],
                                         start=(c == 0), stop=(c == NCH - 1))
                        E.last_inc(S("c_t"), 1)
                    p.on("t", tf)
                    s1 = p.inc("c_s", 1)
                    def sf(E, pssel=pssel, t0=t0,
                           guard=(sev_st[gi2 - 2] if gi2 >= 2 else 0)):
                        E.wait_ge(S("c_t"), t0)
                        if guard:
                            E.wait_ge(S("d_slt" + str(pssel)), guard)
                        E.activation(sev[pssel][:, :], psBC[pssel][:, :],
                                     AF.Identity)
                        E.last_inc(S("c_s"), 1)
                    p.on("s", sf)
                    slot_ev[gi2] = s1
                    stv = p.inc("d_slt" + str(pssel), 16)
                    def yst(E, dc=dc, nh=nh, pssel=pssel, s1=s1):
                        E.wait_ge(S("c_s"), s1)
                        E.dma_start(out=d["slotsT_hbm"][ts(dc), ts(nh, 1024)],
                                    in_=sev[pssel][:, :]
                                    ).then_inc(S("d_slt" + str(pssel)), 16)
                    p.on("y", yst)
                    sev_st[gi2] = stv
                half_done[nh] = p.val("c_s")
            slots_done = p.val("c_s")
            sltv = (p.val("d_slt0"), p.val("d_slt1"))
            if "slotsT" in TAPS:
                p.inc("d_tap", 16)
                def gt(E, sltv=sltv):
                    E.wait_ge(S("d_slt0"), sltv[0])
                    E.wait_ge(S("d_slt1"), sltv[1])
                    E.dma_start(out=d["tap_slotsT"][:, :],
                                in_=d["slotsT_hbm"][:, :]).then_inc(S("d_tap"), 16)
                p.on("g", gt)

            yq_ev = [0] * 16
            wsig = [0] * 64
            bq_lv = [0] * 16
            for n in range(64):
                par = n % 2
                qd, qpos = n // 4, n % 4
                sname = "d_wA" if par == 0 else "d_wB"
                wv = p.inc(sname, 16)
                def gw(E, n=n, par=par, sname=sname, sd=slots_done,
                       guard=(wsig[n - 2] if n >= 2 else 0)):
                    if n < 2:
                        E.wait_ge(S("c_s"), sd)
                    if guard:
                        E.wait_ge(S("c_t"), guard)
                    E.dma_start(out=w0f[par][:, :, :],
                                in_=d["w0"][n, :, :].rearrange("(a p) e -> p a e", p=128)
                                ).then_inc(S(sname), 16)
                p.on("g", gw)
                scv = p.inc("d_sc" + str(par), 16)
                def gsc(E, n=n, par=par, sltv=sltv,
                        guard=(wsig[n - 2] if n >= 2 else 0)):
                    if n < 2:
                        E.wait_ge(S("d_slt0"), sltv[0])
                        E.wait_ge(S("d_slt1"), sltv[1])
                    if guard:
                        E.wait_ge(S("c_t"), guard)
                    E.dma_start(
                        out=scol[par][:, :, :],
                        in_=d["slotsT_hbm"].rearrange("(a p) n -> p a n", p=128)
                        [:, :, n * 32:(n + 1) * 32]
                    ).then_inc(S("d_sc" + str(par)), 16)
                p.on("g", gsc)
                if qpos == 0:
                    bq_lv[qd] = p.inc("d_b0e" + str(qd % 2), 16)
                    def gbq(E, qd=qd, sd=slots_done,
                            guard=(yq_ev[qd - 2] if qd >= 2 else 0)):
                        if qd < 2:
                            E.wait_ge(S("c_s"), sd)
                        if guard:
                            E.wait_ge(S("c_v"), guard)
                        E.dma_start(out=b0q[qd % 2][:, :],
                                    in_=d["b0e"].rearrange("(n p) e -> p n e", p=128)[:, qd, :]
                                    ).then_inc(S("d_b0e" + str(qd % 2)), 16)
                    p.on("g", gbq)
                t0 = p.inc("c_t", 1)
                def tf(E, par=par, qd=qd, qpos=qpos, wv=wv, scv=scv, sname=sname,
                       guard=(yq_ev[qd - 2] if (qd >= 2 and qpos == 0) else 0)):
                    E.wait_ge(S(sname), wv)
                    E.wait_ge(S("d_sc" + str(par)), scv)
                    if guard:
                        E.wait_ge(S("c_v"), guard)
                    ps = psBC[qd % 2]
                    for dc in range(DCH):
                        for j in range(2):
                            E.matmul(ps[32 * qpos:32 * qpos + 32, ts(j, 512)],
                                     scol[par][:, dc, :],
                                     w0f[par][:, dc, ts(j, 512)],
                                     start=(dc == 0), stop=(dc == DCH - 1),
                                     tile_position=(0, 32 * qpos))
                    E.last_inc(S("c_t"), 1)
                p.on("t", tf)
                wsig[n] = t0
                if qpos == 3:
                    v0 = p.inc("c_v", 1)
                    def vf(E, qd=qd, t0=t0, bqv=bq_lv[qd], sltv=sltv):
                        E.wait_ge(S("c_t"), t0)
                        E.wait_ge(S("d_b0e" + str(qd % 2)), bqv)
                        if qd >= 14:
                            E.wait_ge(S("d_slt0"), sltv[0])
                            E.wait_ge(S("d_slt1"), sltv[1])
                        E.tensor_add(y_f[:, qd, :], psBC[qd % 2][:, :],
                                     b0q[qd % 2][:, :])
                        E.last_inc(S("c_v"), 1)
                    p.on("v", vf)
                    yq_ev[qd] = v0
            y_done = p.val("c_v")
            if "y" in TAPS:
                p.inc("d_tap", 16)
                def gt(E, yd=y_done):
                    E.wait_ge(S("c_v"), yd)
                    E.dma_start(out=d["tap_y"].rearrange("(n p) e -> p n e", p=128),
                                in_=y_f[:, :, :]).then_inc(S("d_tap"), 16)
                p.on("g", gt)

            tr_sig = [0] * NCH
            g_sig = [0] * NCH
            u_sig = [0] * NCH
            xst = [0] * NCH
            for mt in range(NCH):
                par = mt % 2
                sname = "d_ez" + str(par)
                lv = p.inc(sname, 48)
                def gld(E, mt=mt, par=par, sname=sname, yd=y_done,
                        sw=((p.val("d_s0"), p.val("d_s1")) if mt < 2 else None),
                        tw=(p.val("d_tap") if (mt < 2 and TAPS) else 0),
                        guard=(u_sig[mt - 2] if mt >= 2 else 0)):
                    if mt < 2:
                        E.wait_ge(S("c_v"), yd)
                        E.wait_ge(S("d_s0"), sw[0])
                        E.wait_ge(S("d_s1"), sw[1])
                        if tw:
                            E.wait_ge(S("d_tap"), tw)
                    if guard:
                        E.wait_ge(S("c_v"), guard)
                    E.dma_start(out=Erl[par][:, :],
                                in_=d["E_hbm"][ts(mt), :]).then_inc(S(sname), 16)
                    E.dma_start(out=x4ch[par][:, :],
                                in_=d["x"].rearrange("(n p) e -> p n e", p=128)[:, mt, :]
                                ).then_inc(S(sname), 16)
                    E.dma_start(out=s4ch[par][:, :],
                                in_=d["s_hbm"].rearrange("(n p) e -> p n e", p=128)[:, mt, :]
                                ).then_inc(S(sname), 16)
                p.on("g", gld)
                t0 = p.inc("c_t", 1)
                def tf(E, mt=mt, par=par, lv=lv, sname=sname,
                       guard=(tr_sig[mt - 2] if mt >= 2 else 0)):
                    E.wait_ge(S(sname), lv)
                    if guard:
                        E.wait_ge(S("c_s"), guard)
                    for b in range(4):
                        for k in range(4):
                            E.matmul(psA[:, b * 512 + k * 128:b * 512 + (k + 1) * 128],
                                     Erl[par][:, ts(4 * b + k)], ident[:, :],
                                     start=True, stop=True, is_transpose=True)
                    E.last_inc(S("c_t"), 1)
                p.on("t", tf)
                s1 = p.inc("c_s", 1)
                def sf(E, par=par, t0=t0):
                    E.wait_ge(S("c_t"), t0)
                    E.activation(combt[par][:, :, :],
                                 psA.rearrange("p (a b) -> p a b", a=NCH),
                                 AF.Identity)
                    E.last_inc(S("c_s"), 1)
                p.on("s", sf)
                tr_sig[mt] = s1
                v0 = p.inc("c_v", 1)
                def vs(E, par=par, lv=lv, sname=sname):
                    E.wait_ge(S(sname), lv)
                    E.reduce_sum(tiny[:, 16 + 4 * par:17 + 4 * par],
                                 Erl[par][:, :], axis=AX)
                    E.drain()
                    E.reciprocal(tiny[:, 17 + 4 * par:18 + 4 * par],
                                 tiny[:, 16 + 4 * par:17 + 4 * par])
                    E.last_inc(S("c_v"), 1)
                p.on("v", vs)
                t1 = p.inc("c_t", 1)
                def tf2(E, par=par, s1=s1,
                        guard=(g_sig[mt - 2] if mt >= 2 else 0)):
                    E.wait_ge(S("c_s"), s1)
                    if guard:
                        E.wait_ge(S("c_s"), guard)
                    ps = psBC[par]
                    for ncc in range(NCH):
                        for j in range(2):
                            E.matmul(ps[:, ts(j, 512)], combt[par][:, ncc, :],
                                     y_f[:, ncc, ts(j, 512)],
                                     start=(ncc == 0), stop=(ncc == NCH - 1))
                    E.last_inc(S("c_t"), 1)
                p.on("t", tf2)
                s2 = p.inc("c_s", 1)
                def sg(E, par=par, t1=t1, v0=v0):
                    E.wait_ge(S("c_t"), t1)
                    E.wait_ge(S("c_v"), v0)
                    E.activation(gch[par][:, :], psBC[par][:, :], AF.Silu,
                                 scale=tiny[:, 17 + 4 * par:18 + 4 * par])
                    E.last_inc(S("c_s"), 1)
                p.on("s", sg)
                g_sig[mt] = s2
                v1 = p.inc("c_v", 1)
                def vu(E, mt=mt, par=par, s2=s2,
                       guard=(xst[mt - 2] if mt >= 2 else 0)):
                    E.wait_ge(S("c_s"), s2)
                    if guard:
                        E.wait_ge(S("d_x1" + str(par)), guard)
                    E.tensor_tensor(o4ch[par][:, :], s4ch[par][:, :],
                                    gch[par][:, :], ALU.mult)
                    E.drain()
                    E.tensor_add(o4ch[par][:, :], x4ch[par][:, :], o4ch[par][:, :])
                    E.drain()
                    E.reduce_sum(sac[:, mt, 0:1], o4ch[par][:, :], axis=AX)
                    E.last_inc(S("c_v"), 1)
                p.on("v", vu)
                u_sig[mt] = v1
                s3 = p.inc("c_s", 1)
                def sq(E, mt=mt, par=par, v1=v1):
                    E.wait_ge(S("c_v"), v1)
                    E.activation(sqd[:, par, :], o4ch[par][:, :], AF.Square,
                                 accum_out=sac[:, mt, 1:2])
                    E.last_inc(S("c_s"), 1)
                p.on("s", sq)
                xv = p.inc("d_x1" + str(par), 16)
                def gst(E, mt=mt, par=par, s3=s3):
                    E.wait_ge(S("c_s"), s3)
                    E.dma_start(out=d["x1_hbm"][ts(mt), :],
                                in_=o4ch[par][:, :]).then_inc(S("d_x1" + str(par)), 16)
                p.on("y", gst)
                xst[mt] = xv
            if "x1" in TAPS:
                a, b = p.val("d_x10"), p.val("d_x11")
                p.inc("d_tap", 16)
                def gt(E, a=a, b=b):
                    E.wait_ge(S("d_x10"), a)
                    E.wait_ge(S("d_x11"), b)
                    E.dma_start(out=d["tap_x1"][:, :],
                                in_=d["x1_hbm"][:, :]).then_inc(S("d_tap"), 16)
                p.on("g", gt)

        def p5():
            a, b = p.val("d_x10"), p.val("d_x11")
            p.inc("c_v", 1)
            def vw(E, a=a, b=b):
                E.wait_ge(S("d_x10"), a)
                E.wait_ge(S("d_x11"), b)
                E.last_inc(S("c_v"), 1)
            p.on("v", vw)
            stats_batched(st1, EPS1)
            rv = p.inc("d_rw", 16)
            def g(E, rv=rv):
                E.dma_start(out=rw_sb[:, :], in_=d["rw"][:, :]).then_inc(S("d_rw"), 16)
            p.on("g", g)
            t0 = p.inc("c_t", 1)
            def t(E, rv=rv):
                E.wait_ge(S("d_rw"), rv)
                for dc in range(DCH):
                    E.matmul(psB[0:128, dc * 8:dc * 8 + 8], rw_sb[:, ts(dc)],
                             ident[0:8, 0:8], start=True, stop=True,
                             is_transpose=True)
                E.last_inc(S("c_t"), 1)
            p.on("t", t)
            s0 = p.inc("c_s", 1)
            def s(E, t0=t0):
                E.wait_ge(S("c_t"), t0)
                E.activation(rwT[:, :, :],
                             psB[:, 0:64].rearrange("p (a b) -> p a b", a=DCH),
                             AF.Identity)
                E.last_inc(S("c_s"), 1)
            p.on("s", s)
            rwdone = p.val("c_s")
            p.inc("c_t", 1)
            def t2(E, s0=s0):
                E.wait_ge(S("c_s"), s0)
                E.last_inc(S("c_t"), 1)
            p.on("t", t2)
            ln_to_T(d["x1_hbm"], st1, xnT, x5ch, xn5, preloaded=None,
                    stats=False, tap=("xn1T" if "xn1T" in TAPS else None),
                    pre_waits=(("d_x10", p.val("d_x10")), ("d_x11", p.val("d_x11"))),
                    router=True)
            tall = p.val("c_t")
            p.inc("c_v", 1)
            def vre(E, tall=tall):
                E.wait_ge(S("c_t"), tall)
                E.tensor_copy(rlograw[:, :, :],
                              psA[:, 0:128].rearrange("p (a b) -> p a b", a=NCH))
                E.last_inc(S("c_v"), 1)
            p.on("v", vre)

        def p7():
            v0 = p.inc("c_v", 1)
            def v(E, cready=p.val("d_const"),
                  sw=(p.val("d_s10"), p.val("d_s11"))):
                E.wait_ge(S("d_const"), cready)
                E.wait_ge(S("d_s10"), sw[0])
                E.wait_ge(S("d_s11"), sw[1])
                E.tensor_tensor(rlog[:, :, :], rlograw[:, :, :],
                                ebias[:, None, :].broadcast_to([128, NCH, NE]),
                                ALU.add)
                E.drain()
                for mt in range(NCH):
                    E.reduce_max(tiny[:, mt:mt + 1], rlog[:, mt, :], axis=AX,
                                 negate=True)
                E.last_inc(S("c_v"), 1)
            p.on("v", v)
            s1 = p.inc("c_s", 1)
            def s2(E, v0=v0):
                E.wait_ge(S("c_v"), v0)
                for mt in range(NCH):
                    E.activation(rsm[:, mt, :], rlog[:, mt, :], AF.Exp,
                                 bias=tiny[:, mt:mt + 1],
                                 accum_out=tiny[:, 16 + mt:17 + mt])
                E.last_inc(S("c_s"), 1)
            p.on("s", s2)
            v1 = p.inc("c_v", 1)
            def v2(E, s1=s1):
                E.wait_ge(S("c_s"), s1)
                E.reciprocal(tiny[:, 32:48], tiny[:, 16:32])
                E.drain()
                for mt in range(NCH):
                    E.tensor_scalar_mul(rsm[:, mt, :], rsm[:, mt, :],
                                        tiny[:, 32 + mt:33 + mt])
                E.drain()
                for mt in range(NCH):
                    E.reduce_max(tiny[:, mt:mt + 1], rsm[:, mt, :], axis=AX)
                E.drain()
                E.memset(tiny[:, 48:64], 1.0)
                E.drain()
                for e in range(NE):
                    E.tensor_tensor(rs2[:, :, 0], rsm[:, :, e], tiny[:, 0:16],
                                    ALU.is_equal)
                    E.drain()
                    E.tensor_tensor(rlog[:, :, e], rs2[:, :, 0], tiny[:, 48:64],
                                    ALU.mult)
                    E.drain()
                    E.tensor_sub(tiny[:, 48:64], tiny[:, 48:64], rlog[:, :, e])
                    E.drain()
                E.scalar_tensor_tensor(rs3[:, :, :], rlog[:, :, :], -1e30,
                                       rsm[:, :, :], ALU.mult, ALU.add)
                E.drain()
                for mt in range(NCH):
                    E.reduce_max(tiny[:, 16 + mt:17 + mt], rs3[:, mt, :], axis=AX)
                E.drain()
                E.memset(tiny[:, 48:64], 1.0)
                E.drain()
                for e in range(NE):
                    E.tensor_tensor(rs2[:, :, 0], rs3[:, :, e], tiny[:, 16:32],
                                    ALU.is_equal)
                    E.drain()
                    E.tensor_tensor(rs4[:, :, e], rs2[:, :, 0], tiny[:, 48:64],
                                    ALU.mult)
                    E.drain()
                    E.tensor_sub(tiny[:, 48:64], tiny[:, 48:64], rs4[:, :, e])
                    E.drain()
                E.tensor_sub(tiny[:, 32:48], tiny[:, 0:16], tiny[:, 16:32])
                E.last_inc(S("c_v"), 1)
            p.on("v", v2)
            s2b = p.inc("c_s", 1)
            def s3(E, v1=v1):
                E.wait_ge(S("c_v"), v1)
                E.activation(tiny[:, 0:16], tiny[:, 32:48], AF.Sigmoid)
                E.last_inc(S("c_s"), 1)
            p.on("s", s3)
            v2b = p.inc("c_v", 1)
            def v3(E, s2b=s2b):
                E.wait_ge(S("c_s"), s2b)
                E.tensor_scalar(tiny[:, 16:32], tiny[:, 0:16], -1.0, 1.0,
                                ALU.mult, ALU.add)
                E.drain()
                E.tensor_tensor(comb[:, :, :], rlog[:, :, :],
                                tiny[:, 0:16][:, :, None].broadcast_to([128, NCH, NE]),
                                ALU.mult)
                E.drain()
                E.tensor_tensor(rs3[:, :, :], rs4[:, :, :],
                                tiny[:, 16:32][:, :, None].broadcast_to([128, NCH, NE]),
                                ALU.mult)
                E.drain()
                E.tensor_add(comb[:, :, :], comb[:, :, :], rs3[:, :, :])
                E.last_inc(S("c_v"), 1)
            p.on("v", v3)
            if "comb" in TAPS:
                p.inc("d_tap", 16)
                def gt(E, v2b=v2b):
                    E.wait_ge(S("c_v"), v2b)
                    E.dma_start(out=d["tap_comb"].rearrange("(n p) e -> p n e", p=128),
                                in_=comb[:, :, :]).then_inc(S("d_tap"), 16)
                p.on("g", gt)

        def p8():
            combv = p.val("c_v")
            castv = p.val("d_castA")
            q_cons = []
            d_cons = []
            oacc_sig = [[0] * NCH for _ in range(19)]
            gbufs = [wA[:, :, 0:512], wA[:, :, 512:1024]]
            ubufs = [wB[:, :, 0:512], wB[:, :, 512:1024]]
            for e in range(9):
                for fh in range(2):
                    rnd = e * 2 + fh
                    wpar = rnd % 2
                    tdv = p.inc("d_td" + str(wpar), 16 * 8)
                    def gtd(E, e=e, fh=fh, wpar=wpar, tdv=tdv, castv=castv,
                            guard=(d_cons[-1][1] if d_cons else 0)):
                        if e == 0 and fh == 0:
                            E.wait_ge(S("d_castA"), castv)
                        if guard:
                            E.wait_ge(S("c_t"), guard)
                        for fc in range(8):
                            fcg = fh * 8 + fc
                            E.dma_start(out=wdThb[wpar][:, fc, :],
                                        in_=d["wd_bf9"][e, :, ts(fcg)],
                                        transpose=True).then_inc(S("d_td" + str(wpar)), 16)
                    p.on("y", gtd)

                    for qq in range(2):
                        iq = rnd * 2 + qq
                        slot = iq % 2
                        tqv = p.inc("d_tq" + str(slot), 16 * 16)
                        def gtq(E, e=e, fh=fh, qq=qq, slot=slot, tqv=tqv,
                                guard=(q_cons[-2][1] if len(q_cons) >= 2 else 0)):
                            if guard:
                                E.wait_ge(S("c_t"), guard)
                            fq = fh * 1024 + qq * 512
                            for dc in range(DCH):
                                E.dma_start(out=gbufs[slot][:, dc, :],
                                            in_=d["wg_bf9"][e, fq:fq + 512, ts(dc)],
                                            transpose=True).then_inc(S("d_tq" + str(slot)), 16)
                                E.dma_start(out=ubufs[slot][:, dc, :],
                                            in_=d["wu_bf9"][e, fq:fq + 512, ts(dc)],
                                            transpose=True).then_inc(S("d_tq" + str(slot)), 16)
                        p.on("y", gtq)

                        last_t = 0
                        for ft in range(4):
                            fc_in_half = qq * 4 + ft
                            for mh in range(2):
                                sil_prev = p.val("c_s")
                                hev_prev = p.val("c_v")
                                t0 = p.inc("c_t", 1)
                                def tg(E, ft=ft, mh=mh, slot=slot,
                                       tqv=tqv, sp=sil_prev, hp=hev_prev,
                                       first=(ft == 0 and mh == 0)):
                                    if first:
                                        E.wait_ge(S("d_tq" + str(slot)), tqv)
                                    if sp:
                                        E.wait_ge(S("c_s"), sp)
                                    if hp:
                                        E.wait_ge(S("c_v"), hp)
                                    fcol = ft * 128
                                    for dc in range(DCH):
                                        for j in range(2):
                                            E.matmul(psA[:, mh * 1024 + j * 512:
                                                          mh * 1024 + (j + 1) * 512],
                                                     gbufs[slot][:, dc, fcol:fcol + 128],
                                                     xnT[:, dc, mh * 1024 + j * 512:
                                                         mh * 1024 + (j + 1) * 512],
                                                     start=(dc == 0), stop=(dc == DCH - 1))
                                    for dc in range(DCH):
                                        for j in range(2):
                                            E.matmul(psBC[mh][:, ts(j, 512)],
                                                     ubufs[slot][:, dc, fcol:fcol + 128],
                                                     xnT[:, dc, mh * 1024 + j * 512:
                                                         mh * 1024 + (j + 1) * 512],
                                                     start=(dc == 0), stop=(dc == DCH - 1))
                                    E.last_inc(S("c_t"), 1)
                                p.on("t", tg)
                                last_t = t0
                                s1 = p.inc("c_s", 1)
                                def sg(E, mh=mh, t0=t0):
                                    E.wait_ge(S("c_t"), t0)
                                    E.activation(silu_g[mh][:, :],
                                                 psA[:, ts(mh, 1024)], AF.Silu)
                                    E.last_inc(S("c_s"), 1)
                                p.on("s", sg)
                                v0 = p.inc("c_v", 1)
                                def vh(E, fc=fc_in_half, mh=mh, s1=s1):
                                    E.wait_ge(S("c_s"), s1)
                                    E.tensor_tensor(hTh[:, fc, ts(mh, 1024)],
                                                    silu_g[mh][:, :],
                                                    psBC[mh][:, :], ALU.mult)
                                    E.last_inc(S("c_v"), 1)
                                p.on("v", vh)
                        q_cons.append(((e, iq), last_t))

                    h_done = p.val("c_v")
                    dn_sig = [0] * NCH
                    last_dn = 0
                    for mt in range(NCH):
                        t0 = p.inc("c_t", 1)
                        def td(E, mt=mt, wpar=wpar, tdv=tdv, hd=h_done,
                               guard=(dn_sig[mt - 2] if mt >= 2 else 0)):
                            if mt == 0:
                                E.wait_ge(S("d_td" + str(wpar)), tdv)
                                E.wait_ge(S("c_v"), hd)
                            if guard:
                                E.wait_ge(S("c_v"), guard)
                            ps = psBC[mt % 2]
                            for fc in range(8):
                                for j in range(2):
                                    E.matmul(ps[:, ts(j, 512)],
                                             hTh[:, fc, ts(mt)],
                                             wdThb[wpar][:, fc, ts(j, 512)],
                                             start=(fc == 0), stop=(fc == 7))
                            E.last_inc(S("c_t"), 1)
                        p.on("t", td)
                        last_dn = t0
                        v0 = p.inc("c_v", 1)
                        def vo(E, e=e, fh=fh, mt=mt, t0=t0, combv=combv,
                               prev=(oacc_sig[rnd - 1][mt] if rnd >= 1 else 0)):
                            E.wait_ge(S("c_t"), t0)
                            if e == 1 and fh == 0 and mt == 0:
                                E.wait_ge(S("c_v"), combv)
                            if prev:
                                E.wait_ge(S("c_v"), prev)
                            ps = psBC[mt % 2]
                            if e == 0 and fh == 0:
                                E.tensor_copy(o_acc[:, mt, :], ps[:, :])
                            elif e == 0:
                                E.tensor_add(o_acc[:, mt, :], o_acc[:, mt, :], ps[:, :])
                            else:
                                E.scalar_tensor_tensor(o_acc[:, mt, :], ps[:, :],
                                                       comb[:, mt, e - 1:e],
                                                       o_acc[:, mt, :],
                                                       ALU.mult, ALU.add)
                            E.last_inc(S("c_v"), 1)
                        p.on("v", vo)
                        dn_sig[mt] = v0
                        oacc_sig[rnd][mt] = v0
                    d_cons.append(((e, fh), last_dn))
            if "oacc" in TAPS:
                vv = p.val("c_v")
                p.inc("d_tap", 16)
                def gt(E, vv=vv):
                    E.wait_ge(S("c_v"), vv)
                    E.dma_start(out=d["tap_oacc"].rearrange("(n p) e -> p n e", p=128),
                                in_=o_acc[:, :, :]).then_inc(S("d_tap"), 16)
                p.on("g", gt)

        def p9():
            ov = p.val("c_v")
            out_sig = [0] * NCH
            vf_sig = [0] * NCH
            for mt in range(NCH):
                par = mt % 2
                sname = "d_ld" + str(par)
                lv = p.inc(sname, 32)
                def gld(E, mt=mt, par=par, sname=sname,
                        pw=(((("d_x10", p.val("d_x10")), ("d_x11", p.val("d_x11")),
                              ("d_s10", p.val("d_s10")), ("d_s11", p.val("d_s11"))))
                            if mt < 2 else ()),
                        guard=(out_sig[mt - 2] if mt >= 2 else 0)):
                    for nm, vv in pw:
                        E.wait_ge(S(nm), vv)
                    if guard:
                        E.wait_ge(S("d_out" + str(par)), guard)
                    E.dma_start(out=x9ch[par][:, :],
                                in_=d["x1_hbm"][ts(mt), :]).then_inc(S(sname), 16)
                    E.dma_start(out=s9ch[par][:, :],
                                in_=d["s1_hbm"][ts(mt), :]).then_inc(S(sname), 16)
                p.on("g", gld)
                s0 = p.inc("c_s", 1)
                def sg(E, mt=mt, par=par, ov=ov,
                       guard=(vf_sig[mt - 2] if mt >= 2 else 0)):
                    E.wait_ge(S("c_v"), ov)
                    if guard:
                        E.wait_ge(S("c_v"), guard)
                    E.activation(g1ch[par][:, :], o_acc[:, mt, :], AF.Silu)
                    E.last_inc(S("c_s"), 1)
                p.on("s", sg)
                v0 = p.inc("c_v", 1)
                def vf(E, mt=mt, par=par, s0=s0, lv=lv, sname=sname):
                    E.wait_ge(S("c_s"), s0)
                    E.wait_ge(S(sname), lv)
                    E.tensor_tensor(o9ch[par][:, :], s9ch[par][:, :],
                                    g1ch[par][:, :], ALU.mult)
                    E.drain()
                    E.tensor_add(o9ch[par][:, :], x9ch[par][:, :], o9ch[par][:, :])
                    E.last_inc(S("c_v"), 1)
                p.on("v", vf)
                vf_sig[mt] = v0
                osv = p.inc("d_out" + str(par), 16)
                def gst(E, mt=mt, par=par, v0=v0):
                    E.wait_ge(S("c_v"), v0)
                    E.dma_start(out=d["out"][ts(mt), :],
                                in_=o9ch[par][:, :]).then_inc(S("d_out" + str(par)), 16)
                p.on("y", gst)
                out_sig[mt] = osv

        p0()
        p.barrier()
        if upto >= 1:
            p1()
            p.barrier()
        if upto >= 2:
            p2()
            p.barrier()
        if upto >= 3:
            attention_l0()
            p.barrier()
        if upto >= 4:
            p4()
            p.barrier()
        if upto >= 5:
            p5()
            p.barrier()
        if upto >= 6:
            attention(1)
            p.barrier()
        if upto >= 7:
            p7()
            p.barrier()
        if upto >= 8:
            p8()
            p.barrier()
        if upto >= 9:
            p9()
            p.barrier()

        def fin():
            waits = []
            for nm in ("d_out0", "d_out1", "d_tap"):
                if nm in p.cnt and p.cnt[nm]:
                    waits.append((nm, p.cnt[nm]))
            def y(E, waits=tuple(waits)):
                for nm, vv in waits:
                    E.wait_ge(S(nm), vv)
            p.on("y", y)
        fin()

        with nc.Block() as block:
            @block.gpsimd
            def _(E):
                p.replay("g", E)

            @block.tensor
            def _(E):
                p.replay("t", E)

            @block.scalar
            def _(E):
                p.replay("s", E)

            @block.vector
            def _(E):
                p.replay("v", E)

            @block.sync
            def _(E):
                p.replay("y", E)

    return nc


def make_inputs(inp):
    f = np.ascontiguousarray
    B = 8
    x = np.asarray(inp["L0_attn_x"], dtype=np.float32)
    wq = f(np.asarray(inp["L0_attn_Wq"]).reshape(D, D))
    wk = f(np.asarray(inp["L0_attn_Wk"]).reshape(D, D))
    wv = f(np.asarray(inp["L0_attn_Wv"]).reshape(D, D))
    wo = f(np.asarray(inp["L0_attn_Wo"]).reshape(D, D))
    bq = np.asarray(inp["L0_attn_bq"]).reshape(D)
    bk = np.asarray(inp["L0_attn_bk"]).reshape(D)
    bv = np.asarray(inp["L0_attn_bv"]).reshape(D)
    bo = np.asarray(inp["L0_attn_bo"]).reshape(D)
    gam = np.asarray(inp["L0_attn_gamma"]).reshape(D)
    bet = np.asarray(inp["L0_attn_beta"]).reshape(D)
    phi2 = f(np.asarray(inp["L0_ffn_phi"]).reshape(D, NP))
    w0 = f(np.asarray(inp["L0_ffn_W"]))
    b0 = np.asarray(inp["L0_ffn_b"])
    wqkv = np.asarray(inp["L1_attn_W_qkv"]).reshape(D, NH, 3, 64)
    bqkv = np.asarray(inp["L1_attn_b_qkv"]).reshape(NH, 3, 64)
    wout = f(np.asarray(inp["L1_attn_W_out"]))
    bout = np.asarray(inp["L1_attn_b_out"]).reshape(D)
    rw = f(np.asarray(inp["L1_ffn_router_w"]))
    ebias = np.asarray(inp["L1_ffn_expert_bias"]).reshape(NE)
    swg = f(np.asarray(inp["L1_ffn_sw_gate"]))
    swu = f(np.asarray(inp["L1_ffn_sw_up"]))
    swd = f(np.asarray(inp["L1_ffn_sw_down"]))
    ewg = f(np.asarray(inp["L1_ffn_ew_gate"]))
    ewu = f(np.asarray(inp["L1_ffn_ew_up"]))
    ewd = f(np.asarray(inp["L1_ffn_ew_down"]))

    col = lambda a: f(a.reshape(8, 128).T)
    bqkcols = np.concatenate(
        [col(bq), col(bk),
         col(f(bqkv[:, 0, :].reshape(D))), col(f(bqkv[:, 1, :].reshape(D)))],
        axis=1).astype(np.float32)
    brows = np.stack([bv, bo, f(bqkv[:, 2, :].reshape(D)), bout]).astype(np.float32)
    gamma_bc = f(np.broadcast_to(gam[None, :], (128, D)).copy())
    beta_bc = f(np.broadcast_to(bet[None, :], (128, D)).copy())
    b0e = f(np.repeat(b0, 32, axis=0))
    ebias_bc = f(np.broadcast_to(ebias[None, :], (128, NE)).copy())
    wq1 = f(wqkv[:, :, 0, :].reshape(D, D))
    wk1 = f(wqkv[:, :, 1, :].reshape(D, D))
    wv1 = f(wqkv[:, :, 2, :].reshape(D, D))

    shared = dict(wq=wq, wk=wk, wv=wv, wo=wo, bqkc=bqkcols, brows=brows,
                  gamma_bc=gamma_bc, beta_bc=beta_bc, phi2=phi2, w0=w0,
                  b0e=b0e, wq1=wq1, wk1=wk1, wv1=wv1, wout=wout, rw=rw,
                  ebias_bc=ebias_bc, swg=swg, swu=swu, swd=swd,
                  ewg=ewg, ewu=ewu, ewd=ewd)
    in_maps = []
    for c in range(B):
        m = dict(shared)
        m["x"] = f(x[c])
        in_maps.append(m)
    return in_maps


def kernel(**inputs):
    in_maps = make_inputs(inputs)
    nc = build()
    res = run_bass_kernel_spmd(nc, in_maps, core_ids=list(range(8)))
    return np.stack([np.asarray(r["out"]) for r in res.results], axis=0)


# version 06 (final)
# speedup vs baseline: 1.0750x; 1.0750x over previous
import numpy as np
from contextlib import ExitStack

import concourse.bass as bass
import concourse.mybir as mybir
from concourse.bass_utils import run_bass_kernel_spmd

F32 = mybir.dt.float32
BF16 = mybir.dt.bfloat16
AF = mybir.ActivationFunctionType
ALU = mybir.AluOpType
AX = mybir.AxisListType.X

M, D, NCH, DCH = 2048, 1024, 16, 8
NP, FF, NE, NH = 2048, 2048, 8, 16
EPS1, EPS2 = 1e-5, 1e-3
ENGS = ("g", "t", "s", "v", "y")


def ts(i, n=128):
    return slice(i * n, (i + 1) * n)


class Prog:
    def __init__(self, nc, es):
        self.nc, self.es = nc, es
        self.items, self.sems, self.cnt = [], {}, {}
        self.nbar = 0

    def sem(self, name):
        if name not in self.sems:
            self.sems[name] = self.es.enter_context(self.nc.semaphore(name))
            self.cnt[name] = 0
        return self.sems[name]

    def inc(self, name, k=1):
        self.sem(name)
        self.cnt[name] += k
        return self.cnt[name]

    def val(self, name):
        self.sem(name)
        return self.cnt[name]

    def on(self, eng, fn):
        self.items.append((eng, fn))

    def barrier(self):
        bar = self.sem("bar")
        self.nbar += 1
        tgt = 5 * self.nbar
        for e in ENGS:
            def _f(E, tgt=tgt, bar=bar):
                E.drain().then_inc(bar, 1)
                E.wait_ge(bar, tgt)
            self.on(e, _f)

    def replay(self, eng, E):
        w = EngWrap(E)
        for e, f in self.items:
            if e == eng:
                f(w)


class EngWrap:
    _PASS = {"drain", "wait_ge", "nop"}

    def __init__(self, E):
        self.__dict__["_E"] = E
        self.__dict__["last"] = None

    def __getattr__(self, nm):
        a = getattr(self._E, nm)
        if callable(a):
            def w(*args, **kw):
                r = a(*args, **kw)
                if nm not in EngWrap._PASS and hasattr(r, "then_inc"):
                    self.__dict__["last"] = r
                return r
            return w
        return a

    def last_inc(self, sem, n=1):
        last = self.__dict__["last"]
        self.__dict__["last"] = None
        if last is None:
            return self._E.drain().then_inc(sem, n)
        return last.then_inc(sem, n)


def build(taps=(), upto=99):
    nc = bass.Bass("TRN2", target_bir_lowering=False)
    d = {}

    def din(name, shape, dt=F32):
        d[name] = nc.dram_tensor(name, list(shape), dt, kind="ExternalInput").ap()

    def dout(name, shape, dt=F32):
        d[name] = nc.dram_tensor(name, list(shape), dt, kind="ExternalOutput").ap()

    def dint(name, shape, dt=F32):
        d[name] = nc.dram_tensor(name, list(shape), dt).ap()

    din("x", (M, D))
    din("wq", (D, D)); din("wk", (D, D)); din("wv", (D, D)); din("wo", (D, D))
    din("bqkc", (128, 32)); din("brows", (4, D))
    din("gamma_bc", (128, D)); din("beta_bc", (128, D))
    din("phi2", (D, NP)); din("w0", (64, D, D)); din("b0e", (NP, D))
    din("wq1", (D, D)); din("wk1", (D, D)); din("wv1", (D, D)); din("wout", (D, D))
    din("rw", (NE, D)); din("ebias_bc", (128, NE))
    din("swg", (FF, D)); din("swu", (FF, D)); din("swd", (D, FF))
    din("ewg", (NE, FF, D)); din("ewu", (NE, FF, D)); din("ewd", (NE, D, FF))
    dout("out", (M, D))

    dint("E_hbm", (M, NP)); dint("Z_hbm", (M, NP))
    dint("s_hbm", (M, D)); dint("s1_hbm", (M, D)); dint("x1_hbm", (M, D))
    dint("q_hbm", (8, 128, M)); dint("k_hbm", (8, 128, M))
    dint("v_hbm", (8, 128, NCH * 2 * 66))
    dint("slotsT_hbm", (D, NP))
    dint("wg_bf9", (9, FF, D), BF16)
    dint("wu_bf9", (9, FF, D), BF16)
    dint("wd_bf9", (9, D, FF), BF16)

    TAPS = set(taps)
    tapshape = {"xnT": (D, M), "E": (M, NP), "s": (M, D), "x1": (M, D),
                "s1": (M, D), "slotsT": (D, NP), "y": (NP, D),
                "comb": (M, NE), "oacc": (M, D), "xn1T": (D, M)}
    for t in TAPS:
        dout("tap_" + t, tapshape[t])

    with ExitStack() as es:
        p = Prog(nc, es)
        S = p.sem
        sb = es.enter_context

        ident = nc.alloc_sbuf_tensor("ident", [128, 128], F32).ap()
        ones = nc.alloc_sbuf_tensor("ones", [128, 128], BF16).ap()
        onesf = nc.alloc_sbuf_tensor("onesf", [1, 128], F32).ap()
        bqkc = nc.alloc_sbuf_tensor("bqkc_sb", [128, 32], F32).ap()
        brow = nc.alloc_sbuf_tensor("brow", [1, 2, D], BF16).ap()
        ebias = nc.alloc_sbuf_tensor("ebias_sb", [128, NE], F32).ap()
        st0 = nc.alloc_sbuf_tensor("st0", [128, NCH, 2], F32).ap()
        st1 = nc.alloc_sbuf_tensor("st1", [128, NCH, 2], F32).ap()
        sac = nc.alloc_sbuf_tensor("sac", [128, NCH, 2], F32).ap()
        tiny = nc.alloc_sbuf_tensor("tiny", [128, 64], F32).ap()
        comb = nc.alloc_sbuf_tensor("comb", [128, NCH, NE], F32).ap()
        rlograw = nc.alloc_sbuf_tensor("rlograw", [128, NCH, NE], F32).ap()
        rowsc = nc.alloc_sbuf_tensor("rowsc", [1, 4, 512], F32).ap()
        sqd = nc.alloc_sbuf_tensor("sqd", [128, 2, D], BF16).ap()
        wA = nc.alloc_sbuf_tensor("wA", [128, 8, 1024], BF16).ap()
        wB = nc.alloc_sbuf_tensor("wB", [128, 8, 1024], BF16).ap()

        AR = 153 * 1024
        arena_t = nc.alloc_sbuf_tensor("arena", [128, AR], mybir.dt.uint8)
        ar0 = nc.lookup_mloc(arena_t).addr

        def at(name, off_kb, shape, dt):
            h = nc.alloc_sbuf_tensor_at(name, list(shape), dt,
                                        offset=ar0 + int(off_kb * 1024))
            return h.ap()

        xnTf = at("xnTf", 0, [128, DCH, M], F32)
        x_sb = at("x_sb", 64, [128, NCH, D], F32)
        xnf = [at(f"xnf{i}", 128 + 4 * i, [128, D], F32) for i in range(2)]
        phi_f = at("phi_f", 64, [128, DCH, NP], F32)
        Ech = [at(f"Ech{i}", 128 + 8 * i, [128, NP], F32) for i in range(2)]
        wqp = [at(f"wqp{i}", 128 + 4 * i, [128, 8, 128], F32) for i in range(2)]
        vsp = at("vsp", 136, [128, NCH, 2, 66], F32)
        qkev = at("qkev", 64, [128, 2048], F32)
        qTh = [at(f"qTh{i}", 0 + 4 * i, [128, 1024], F32) for i in range(2)]
        kTp = [at(f"kTp{i}", 8 + 8 * i, [128, 2048], F32) for i in range(2)]
        vp = [at(f"vp{i}", 24 + 9 * i, [128, NCH, 2, 66], F32) for i in range(2)]
        attnT0 = [at(f"attnT0{i}", 42 + 8 * i, [128, 2048], F32) for i in range(2)]
        ctxTh = at("ctxTh", 58, [128, DCH, 1024], F32)
        bcf = at("bcf", 90, [128, 2048], F32)
        bvf = at("bvf", 98, [1, 2, D], F32)
        wof = at("wof", 8, [128, DCH, 1024], F32)
        xch3 = at("xch3", 106, [128, D], F32)
        xnch3 = at("xnch3", 110, [128, D], F32)
        ych3 = at("ych3", 114, [128, D], F32)
        zch3 = at("zch3", 118, [128, D], F32)
        sch3 = [at(f"sch3{i}", 122 + 4 * i, [128, D], F32) for i in range(2)]
        gb_sb = at("gb_sb", 130, [128, 2, D], F32)
        xn_tok = at("xn_tok", 0, [128, NCH, D], F32)
        disp = at("disp", 64, [128, NCH, 1024], F32)
        Ezb = [at(f"Ezb{i}", 128 + 4 * i, [128, 1024], F32) for i in range(2)]
        Zzb = [at(f"Zzb{i}", 136 + 4 * i, [128, 1024], F32) for i in range(2)]
        sev = [at(f"sev{i}", 144 + 4 * i, [128, 1024], F32) for i in range(2)]
        w0f = [at(f"w0f{i}", 0 + 32 * i, [128, DCH, 1024], F32) for i in range(2)]
        scol = [at(f"scol{i}", 64 + 1 * i, [128, DCH, 32], F32) for i in range(2)]
        b0q = [at(f"b0q{i}", 68 + 4 * i, [128, D], F32) for i in range(2)]
        y_f = at("y_f", 88, [128, NCH, D], F32)
        Erl = [at(f"Erl{i}", 0 + 8 * i, [128, NP], F32) for i in range(2)]
        combt = [at(f"combt{i}", 16 + 8 * i, [128, NCH, 128], F32) for i in range(2)]
        gch = [at(f"gch{i}", 32 + 4 * i, [128, D], F32) for i in range(2)]
        x4ch = [at(f"x4ch{i}", 40 + 4 * i, [128, D], F32) for i in range(2)]
        s4ch = [at(f"s4ch{i}", 48 + 4 * i, [128, D], F32) for i in range(2)]
        o4ch = [at(f"o4ch{i}", 56 + 4 * i, [128, D], F32) for i in range(2)]
        xnT = at("xnT", 0, [128, DCH, M], BF16)
        x5ch = [at(f"x5ch{i}", 32 + 4 * i, [128, D], F32) for i in range(2)]
        xn5 = [at(f"xn5{i}", 40 + 4 * i, [128, D], F32) for i in range(2)]
        rw_sb = at("rw_sb", 48, [8, D], F32)
        trf32 = at("trf32", 52, [128, DCH, 128], F32)
        rwT = at("rwT", 56, [128, DCH, NE], F32)
        qTg = at("qTg", 32, [128, 4, M], BF16)
        kTg = at("kTg", 48, [128, 4, M], BF16)
        v_att = at("v_att", 64, [128, NCH, NH, 66], BF16)
        attnT = [at(f"attnT{i}", 97 + 4 * i, [128, 2048], BF16) for i in range(2)]
        ctxT = at("ctxT", 105, [128, DCH, M], BF16)
        bcf1 = at("bcf1", 137, [128, 2048], F32)
        sch = [at(f"sch{i}", 145 + 4 * i, [128, D], F32) for i in range(2)]
        xch = [at(f"xch{i}", 0 + 4 * i, [128, D], F32) for i in range(2)]
        ych = [at(f"ych{i}", 8 + 4 * i, [128, D], F32) for i in range(2)]
        zch = [at(f"zch{i}", 16 + 4 * i, [128, D], F32) for i in range(2)]
        xnf2 = [at(f"xnf2{i}", 24 + 4 * i, [128, D], F32) for i in range(2)]
        rlog = at("rlog", 32, [128, NCH, NE], F32)
        rsm = at("rsm", 34, [128, NCH, NE], F32)
        rs2 = at("rs2", 36, [128, NCH, NE], F32)
        rs3 = at("rs3", 38, [128, NCH, NE], F32)
        rs4 = at("rs4", 40, [128, NCH, NE], F32)
        o_acc = at("o_acc", 32, [128, NCH, D], F32)
        hTh = at("hTh", 96, [128, 8, M], BF16)
        wdTh = at("wdTh", 128, [128, 8, D], BF16)
        wdThb = [wdTh, wdTh]
        silu_g = [at(f"silu_g{i}", 144 + 2 * i, [128, 1024], BF16) for i in range(2)]
        g1ch = [at(f"g1ch{i}", 96 + 4 * i, [128, D], F32) for i in range(2)]
        x9ch = [at(f"x9ch{i}", 104 + 4 * i, [128, D], F32) for i in range(2)]
        s9ch = [at(f"s9ch{i}", 112 + 4 * i, [128, D], F32) for i in range(2)]
        o9ch = [at(f"o9ch{i}", 120 + 4 * i, [128, D], F32) for i in range(2)]

        psA = sb(nc.psum_tensor("psA", [128, 2048], F32))
        psB = sb(nc.psum_tensor("psB", [128, 1024], F32))
        psC = sb(nc.psum_tensor("psC", [128, 1024], F32))
        psBC = [psB, psC]

        def p0():
            p.inc("d_const", 16 * 3)
            def g(E):
                E.dma_start(out=bqkc[:, :], in_=d["bqkc"][:, :]).then_inc(S("d_const"), 16)
                E.dma_start(out=brow[0:1, :, :], in_=d["brows"][None, 2:4, :]).then_inc(S("d_const"), 16)
                E.dma_start(out=ebias[:, :], in_=d["ebias_bc"][:, :]).then_inc(S("d_const"), 16)
                E.memset(ident[:, :], 0.0)
                E.memset(ones[:, :], 1.0)
                E.memset(onesf[:, :], 1.0)
                E.drain()
                E.affine_select(out=ident[:, :], in_=ident[:, :],
                                compare_op=ALU.not_equal, fill=1.0,
                                base=0, pattern=[[-1, 128]], channel_multiplier=1)
                E.last_inc(S("c_g"), 1)
            p.on("g", g)
            p.inc("c_g", 1)

        def stats_batched(st, eps):
            p.inc("c_v", 1)
            def v(E, st=st, eps=eps):
                E.tensor_scalar_mul(tiny[:, 0:16], sac[:, :, 0], 1.0 / D)
                E.tensor_scalar_mul(tiny[:, 16:32], sac[:, :, 1], 1.0 / D)
                E.drain()
                E.tensor_tensor(tiny[:, 32:48], tiny[:, 0:16], tiny[:, 0:16], ALU.mult)
                E.drain()
                E.tensor_sub(tiny[:, 32:48], tiny[:, 16:32], tiny[:, 32:48])
                E.drain()
                E.tensor_scalar_add(tiny[:, 32:48], tiny[:, 32:48], float(eps))
                E.drain()
                E.reciprocal(tiny[:, 32:48], tiny[:, 32:48])
                E.tensor_scalar_mul(tiny[:, 48:64], tiny[:, 0:16], -1.0)
                E.last_inc(S("c_v"), 1)
            p.on("v", v)
            s0 = p.inc("c_s", 1)
            vv = p.val("c_v")
            def s(E, vv=vv, st=st):
                E.wait_ge(S("c_v"), vv)
                E.activation(st[:, :, 0], tiny[:, 32:48], AF.Sqrt)
                E.last_inc(S("c_s"), 1)
            p.on("s", s)
            p.inc("c_v", 1)
            def v2(E, s0=s0, st=st):
                E.wait_ge(S("c_s"), s0)
                E.tensor_tensor(st[:, :, 1], tiny[:, 48:64], st[:, :, 0], ALU.mult)
                E.last_inc(S("c_v"), 1)
            p.on("v", v2)

        def ln_to_T(src_hbm, st, dstT, chbuf, appbuf, preloaded=None,
                    stats=True, eps=EPS1, dsem="d_ld", tap=None, pre_waits=(),
                    router=False):
            if stats:
                assert preloaded is not None
                xv = p.val("d_x")
                p.inc("c_v", 1)
                def v(E, xv=xv):
                    E.wait_ge(S("d_x"), xv)
                    for c in range(NCH):
                        E.reduce_sum(sac[:, c, 0:1], preloaded[:, c, :], axis=AX)
                    E.last_inc(S("c_v"), 1)
                p.on("v", v)
                s0 = p.inc("c_s", 1)
                def s(E, xv=xv):
                    E.wait_ge(S("d_x"), xv)
                    for c in range(NCH):
                        E.activation(sqd[:, c % 2, :], preloaded[:, c, :],
                                     AF.Square, accum_out=sac[:, c, 1:2])
                        E.drain()
                    E.last_inc(S("c_s"), 1)
                p.on("s", s)
                p.inc("c_v", 1)
                def vj(E, s0=s0):
                    E.wait_ge(S("c_s"), s0)
                    E.last_inc(S("c_v"), 1)
                p.on("v", vj)
                stats_batched(st, eps)
            stdone = p.val("c_v")
            ap_sig = [0] * NCH
            ev_sig = [0] * NCH
            rtr_sig = [0] * NCH
            lvals = [0] * NCH
            for c in range(NCH):
                par = c % 2
                if preloaded is None:
                    sname = dsem + str(par)
                    lvals[c] = p.inc(sname, 16)
                    def gld(E, c=c, par=par, sname=sname,
                            pw=(tuple(pre_waits) if c < 2 else ()),
                            guard=(ap_sig[c - 2] if c >= 2 else 0)):
                        for nm, vv in pw:
                            E.wait_ge(S(nm), vv)
                        if guard:
                            E.wait_ge(S("c_s"), guard)
                        E.dma_start(out=chbuf[par][:, :],
                                    in_=src_hbm.rearrange("(n p) e -> p n e", p=128)[:, c, :]
                                    ).then_inc(S(sname), 16)
                    p.on("g", gld)
                sv = p.inc("c_s", 1)
                def sf(E, c=c, par=par, stdone=stdone, lv=lvals[c],
                       guard=(ev_sig[c - 2] if c >= 2 else 0), dsem=dsem):
                    E.wait_ge(S("c_v"), stdone)
                    if preloaded is None:
                        E.wait_ge(S(dsem + str(par)), lv)
                    if guard:
                        E.wait_ge(S("c_v"), guard)
                    src = (preloaded[:, c, :] if preloaded is not None
                           else chbuf[par][:, :])
                    E.activation(appbuf[par][:, :], src, AF.Identity,
                                 bias=st[:, c, 1:2], scale=st[:, c, 0:1])
                    E.last_inc(S("c_s"), 1)
                p.on("s", sf)
                ap_sig[c] = sv
                t0 = p.inc("c_t", 1)
                def tf(E, c=c, par=par, sv=sv,
                       guard=(ev_sig[c - 2] if c >= 2 else 0)):
                    E.wait_ge(S("c_s"), sv)
                    if guard:
                        E.wait_ge(S("c_v"), guard)
                    ps = psBC[par]
                    for dc in range(DCH):
                        E.matmul(ps[:, ts(dc)], appbuf[par][:, ts(dc)],
                                 ident[:, :], start=True, stop=True,
                                 is_transpose=True)
                    E.last_inc(S("c_t"), 1)
                p.on("t", tf)
                v1 = p.inc("c_v", 1)
                def vf(E, c=c, par=par, t0=t0, router=router,
       rguard=(rtr_sig[c - 1] if (router and c >= 1) else 0)):
                    E.wait_ge(S("c_t"), t0)
                    E.tensor_copy(dstT[:, :, ts(c)],
                                  psBC[par].rearrange("p (a b) -> p a b", a=DCH))
                    if router:
                        if rguard:
                            E.wait_ge(S("c_t"), rguard)
                        E.drain()
                        E.tensor_copy(trf32[:, :, :],
                                      psBC[par].rearrange("p (a b) -> p a b", a=DCH))
                    E.last_inc(S("c_v"), 1)
                p.on("v", vf)
                ev_sig[c] = v1
                if router:
                    rt0 = p.inc("c_t", 1)
                    def trt(E, c=c, v1=v1):
                        E.wait_ge(S("c_v"), v1)
                        for dc in range(DCH):
                            E.matmul(psA[:, c * 8:c * 8 + 8], trf32[:, dc, :],
                                     rwT[:, dc, :], start=(dc == 0),
                                     stop=(dc == DCH - 1))
                        E.last_inc(S("c_t"), 1)
                    p.on("t", trt)
                    rtr_sig[c] = rt0
            if tap:
                vv = p.val("c_v")
                p.inc("d_tap", 16)
                def gt(E, vv=vv, tap=tap):
                    E.wait_ge(S("c_v"), vv)
                    E.dma_start(out=d["tap_" + tap].rearrange("(a p) m -> p a m", p=128),
                                in_=dstT[:, :, :]).then_inc(S("d_tap"), 16)
                p.on("g", gt)

        def p1():
            p.inc("d_x", 16)
            def g(E):
                E.dma_start(out=x_sb[:, :, :],
                            in_=d["x"].rearrange("(n p) e -> p n e", p=128)
                            ).then_inc(S("d_x"), 16)
            p.on("g", g)
            ln_to_T(d["x"], st0, xnTf, None, xnf, preloaded=x_sb,
                    tap=("xnT" if "xnT" in TAPS else None))

        def p2():
            pv = p.inc("d_phi", 16)
            def g(E, pv=pv):
                E.dma_start(out=phi_f[:, :, :],
                            in_=d["phi2"].rearrange("(a p) n -> p a n", p=128)
                            ).then_inc(S("d_phi"), 16)
            p.on("g", g)
            exp_sig = [0] * NCH
            stv = [0] * NCH
            for mt in range(NCH):
                par = mt % 2
                t0 = p.inc("c_t", 1)
                def t(E, mt=mt, pv=pv, guard=(exp_sig[mt - 1] if mt >= 1 else 0)):
                    if mt == 0:
                        E.wait_ge(S("d_phi"), pv)
                    if guard:
                        E.wait_ge(S("c_s"), guard)
                    for dc in range(DCH):
                        for j in range(4):
                            E.matmul(psA[:, ts(j, 512)], xnTf[:, dc, ts(mt)],
                                     phi_f[:, dc, ts(j, 512)],
                                     start=(dc == 0), stop=(dc == DCH - 1))
                    E.last_inc(S("c_t"), 1)
                p.on("t", t)
                sname = "d_E" + str(par)
                s1 = p.inc("c_s", 1)
                def sf(E, mt=mt, par=par, t0=t0, sname=sname,
                       guard=(stv[mt - 2] if mt >= 2 else 0)):
                    E.wait_ge(S("c_t"), t0)
                    if guard:
                        E.wait_ge(S(sname), guard)
                    E.activation(Ech[par][:, :], psA[:, :], AF.Exp)
                    E.last_inc(S("c_s"), 1)
                p.on("s", sf)
                exp_sig[mt] = s1
                ev = p.inc(sname, 16)
                def y(E, mt=mt, par=par, s1=s1, sname=sname):
                    E.wait_ge(S("c_s"), s1)
                    E.dma_start(out=d["E_hbm"][ts(mt), :],
                                in_=Ech[par][:, :]).then_inc(S(sname), 16)
                p.on("y", y)
                stv[mt] = ev
            e0, e1 = p.val("d_E0"), p.val("d_E1")
            p.inc("cc", 1)
            def gcc(E, e0=e0, e1=e1):
                E.wait_ge(S("d_E0"), e0)
                E.wait_ge(S("d_E1"), e1)
                E.collective_compute(
                    "AllReduce", ALU.add, replica_groups=[list(range(8))],
                    ins=[d["E_hbm"][:, :]], outs=[d["Z_hbm"][:, :]],
                ).then_inc(S("cc"), 1)
            p.on("g", gcc)
            if "E" in TAPS:
                p.inc("d_tap", 16)
                def gt(E, e0=e0, e1=e1):
                    E.wait_ge(S("d_E0"), e0)
                    E.wait_ge(S("d_E1"), e1)
                    E.dma_start(out=d["tap_E"][:, :],
                                in_=d["E_hbm"][:, :]).then_inc(S("d_tap"), 16)
                p.on("g", gt)

        def attention_l0():
            cready = p.val("d_const")
            e0, e1 = p.val("d_E0"), p.val("d_E1")
            bvv = p.inc("d_rw", 16)
            p.inc("c_g", 1)
            def g0(E, bvv=bvv, e0=e0, e1=e1):
                E.wait_ge(S("d_E0"), e0)
                E.wait_ge(S("d_E1"), e1)
                E.dma_start(out=bvf[0:1, :, :],
                            in_=d["brows"][None, 0:2, :]).then_inc(S("d_rw"), 16)
                E.memset(vsp[:, :, :, 64:66], 0.0)
                E.drain()
                E.memset(vsp[:, :, :, 64:65], 1.0)
                E.last_inc(S("c_g"), 1)
            p.on("g", g0)
            gv = p.val("c_g")

            wsig = []
            qs_sig = []
            vs_sig = []
            for pr in range(8):
                for wi, wname in enumerate(("wq", "wk", "wv")):
                    par = (pr * 3 + wi) % 2
                    sname = "d_wA" if par == 0 else "d_wB"
                    lv = p.inc(sname, 16)
                    def gw(E, wname=wname, par=par, pr=pr, sname=sname,
                           guard=(wsig[-2] if len(wsig) >= 2 else 0),
                           ew=(len(wsig) < 2), e0=e0, e1=e1):
                        if ew:
                            E.wait_ge(S("d_E0"), e0)
                            E.wait_ge(S("d_E1"), e1)
                        if guard:
                            E.wait_ge(S("c_t"), guard)
                        E.dma_start(out=wqp[par][:, :, :],
                                    in_=d[wname].rearrange("(a p) f -> p a f", p=128)[:, :, ts(pr)]
                                    ).then_inc(S(sname), 16)
                    p.on("g", gw)
                    if wi < 2:
                        col = (0 if wi == 0 else 8) + pr
                        t0 = p.inc("c_t", 1)
                        def t(E, par=par, lv=lv, sname=sname, sprev=p.val("c_s")):
                            E.wait_ge(S(sname), lv)
                            if sprev:
                                E.wait_ge(S("c_s"), sprev)
                            for dc in range(DCH):
                                for j in range(4):
                                    E.matmul(psA[:, ts(j, 512)],
                                             wqp[par][:, dc, :],
                                             xnTf[:, dc, ts(j, 512)],
                                             start=(dc == 0), stop=(dc == DCH - 1))
                            E.last_inc(S("c_t"), 1)
                        p.on("t", t)
                        wsig.append(t0)
                        s0 = p.inc("c_s", 1)
                        def sf(E, col=col, t0=t0, cready=cready,
                               guard=(qs_sig[-1] if qs_sig else 0)):
                            E.wait_ge(S("c_t"), t0)
                            E.wait_ge(S("d_const"), cready)
                            if guard:
                                E.wait_ge(S("d_qs"), guard)
                            E.activation(qkev[:, :], psA[:, :], AF.Identity,
                                         bias=bqkc[:, col:col + 1])
                            E.last_inc(S("c_s"), 1)
                        p.on("s", sf)
                        qv = p.inc("d_qs", 16)
                        def y0(E, pr=pr, wi=wi, s0=s0):
                            E.wait_ge(S("c_s"), s0)
                            E.dma_start(out=d["q_hbm" if wi == 0 else "k_hbm"][pr, :, :],
                                        in_=qkev[:, :]).then_inc(S("d_qs"), 16)
                        p.on("y", y0)
                        qs_sig.append(qv)
                    else:
                        ev_sig = [0] * NCH
                        last_t = 0
                        for mt in range(NCH):
                            t0 = p.inc("c_t", 1)
                            def t(E, mt=mt, par=par, lv=lv, sname=sname, pr=pr,
                                  gv=gv, bvv=bvv,
                                  guard=(ev_sig[mt - 2] if mt >= 2 else 0)):
                                if mt == 0:
                                    E.wait_ge(S(sname), lv)
                                    E.wait_ge(S("c_g"), gv)
                                    E.wait_ge(S("d_rw"), bvv)
                                if guard:
                                    E.wait_ge(S("c_s"), guard)
                                ps = psBC[mt % 2]
                                for dc in range(DCH):
                                    E.matmul(ps[:, 0:128], xnTf[:, dc, ts(mt)],
                                             wqp[par][:, dc, :],
                                             start=(dc == 0), stop=False)
                                E.matmul(ps[:, 0:128], onesf[0:1, 0:128],
                                         bvf[0:1, 0, ts(pr)], start=False, stop=True)
                                E.last_inc(S("c_t"), 1)
                            p.on("t", t)
                            last_t = p.val("c_t")
                            s1 = p.inc("c_s", 1)
                            def sv(E, mt=mt, t0=t0,
                                   guard=(vs_sig[-1] if (mt == 0 and vs_sig) else 0)):
                                E.wait_ge(S("c_t"), t0)
                                if guard:
                                    E.wait_ge(S("d_vs"), guard)
                                E.activation(vsp[:, mt, 0:2, 0:64],
                                             psBC[mt % 2][:, 0:128], AF.Identity)
                                E.last_inc(S("c_s"), 1)
                            p.on("s", sv)
                            ev_sig[mt] = s1
                        wsig.append(last_t)
                        vv = p.inc("d_vs", 16)
                        sall = p.val("c_s")
                        def y1(E, pr=pr, sall=sall):
                            E.wait_ge(S("c_s"), sall)
                            E.dma_start(out=d["v_hbm"][pr, :, :],
                                        in_=vsp.rearrange("p a b c -> p (a b c)")
                                        ).then_inc(S("d_vs"), 16)
                        p.on("y", y1)
                        vs_sig.append(vv)
            qsv, vsv = p.val("d_qs"), p.val("d_vs")

            p.inc("d_castA", 27 * 16)
            def gcast(E):
                for e in range(9):
                    if e == 0:
                        sg_, su_, sd_ = d["swg"][:, :], d["swu"][:, :], d["swd"][:, :]
                    else:
                        sg_, su_, sd_ = (d["ewg"][e - 1, :, :], d["ewu"][e - 1, :, :],
                                         d["ewd"][e - 1, :, :])
                    E.dma_start(out=d["wg_bf9"][e, :, :], in_=sg_).then_inc(S("d_castA"), 16)
                    E.dma_start(out=d["wu_bf9"][e, :, :], in_=su_).then_inc(S("d_castA"), 16)
                    E.dma_start(out=d["wd_bf9"][e, :, :], in_=sd_).then_inc(S("d_castA"), 16)
            p.on("g", gcast)

            prev_ve = 0
            outproj_done = 0
            st_sig = [0] * NCH
            gbv = 0
            for qh in range(2):
                ld_sig = []
                for pr in range(8):
                    par = pr % 2
                    sname = "d_ld" + str(par)
                    lv = p.inc(sname, 48)
                    def gl(E, pr=pr, qh=qh, par=par, sname=sname, qsv=qsv, vsv=vsv,
                           guard=(ld_sig[-2] if len(ld_sig) >= 2 else 0),
                           opd=(outproj_done if (qh == 1 and pr < 2) else 0)):
                        E.wait_ge(S("d_qs"), qsv)
                        E.wait_ge(S("d_vs"), vsv)
                        if opd:
                            E.wait_ge(S("c_t"), opd)
                        if guard:
                            E.wait_ge(S("c_t"), guard)
                        E.dma_start(out=qTh[par][:, :],
                                    in_=d["q_hbm"][pr, :, ts(qh, 1024)]).then_inc(S(sname), 16)
                        E.dma_start(out=kTp[par][:, :],
                                    in_=d["k_hbm"][pr, :, :]).then_inc(S(sname), 16)
                        E.dma_start(out=vp[par].rearrange("p a b c -> p (a b c)"),
                                    in_=d["v_hbm"][pr, :, :]).then_inc(S(sname), 16)
                    p.on("g", gl)
                    exp_sig = [0] * NCH
                    last_ctx = 0
                    for scn in range(NCH):
                        apar = scn % 2
                        t0 = p.inc("c_t", 1)
                        def t(E, scn=scn, par=par, lv=lv, sname=sname,
                              guard=(exp_sig[scn - 1] if scn >= 1 else 0),
                              pve=prev_ve):
                            if scn == 0:
                                E.wait_ge(S(sname), lv)
                                if pve:
                                    E.wait_ge(S("c_v"), pve)
                            if guard:
                                E.wait_ge(S("c_s"), guard)
                            for j in range(2):
                                E.matmul(psA[:, ts(j, 512)],
                                         kTp[par][0:64, ts(scn)],
                                         qTh[par][0:64, ts(j, 512)],
                                         start=True, stop=True, tile_position=(0, 0))
                                E.matmul(psA[:, 1024 + j * 512:1536 + j * 512],
                                         kTp[par][64:128, ts(scn)],
                                         qTh[par][64:128, ts(j, 512)],
                                         start=True, stop=True, tile_position=(64, 0))
                            E.last_inc(S("c_t"), 1)
                        p.on("t", t)
                        s1 = p.inc("c_s", 1)
                        def sf(E, apar=apar, t0=t0):
                            E.wait_ge(S("c_t"), t0)
                            E.activation(attnT0[apar][:, :], psA[:, :],
                                         AF.Exp, scale=0.125)
                            E.last_inc(S("c_s"), 1)
                        p.on("s", sf)
                        exp_sig[scn] = s1
                        t1 = p.inc("c_t", 1)
                        def tc(E, scn=scn, apar=apar, par=par, s1=s1, pve=prev_ve):
                            E.wait_ge(S("c_s"), s1)
                            if scn == 0 and pve:
                                E.wait_ge(S("c_v"), pve)
                            first, last = scn == 0, scn == NCH - 1
                            for hh in range(2):
                                for j in range(2):
                                    E.matmul(psBC[hh][0:65, ts(j, 512)],
                                             vp[par][:, scn, hh, 0:65],
                                             attnT0[apar][:, hh * 1024 + j * 512:
                                                          hh * 1024 + (j + 1) * 512],
                                             start=first, stop=last)
                            E.last_inc(S("c_t"), 1)
                        p.on("t", tc)
                        last_ctx = t1
                    ld_sig.append(last_ctx)
                    tctx = p.val("c_t")
                    v0 = p.inc("c_v", 1)
                    def vd(E, tctx=tctx):
                        E.wait_ge(S("c_t"), tctx)
                        E.tensor_copy(rowsc[0:1, 0:2, :],
                                      psB[64:65, :].rearrange("p (a b) -> p a b", a=2))
                        E.tensor_copy(rowsc[0:1, 2:4, :],
                                      psC[64:65, :].rearrange("p (a b) -> p a b", a=2))
                        E.drain()
                        E.reciprocal(rowsc[0:1, :, :], rowsc[0:1, :, :])
                        E.last_inc(S("c_v"), 1)
                    p.on("v", vd)
                    t2 = p.inc("c_t", 1)
                    def tb(E, v0=v0):
                        E.wait_ge(S("c_v"), v0)
                        for j in range(4):
                            E.matmul(psA[0:64, ts(j, 512)], onesf[0:1, 0:64],
                                     rowsc[0:1, j, :], start=True, stop=True)
                        E.last_inc(S("c_t"), 1)
                    p.on("t", tb)
                    v1 = p.inc("c_v", 1)
                    def ve(E, pr=pr, t2=t2):
                        E.wait_ge(S("c_t"), t2)
                        E.tensor_copy(bcf[0:64, :], psA[0:64, :])
                        E.drain()
                        for hh in range(2):
                            hg = 2 * pr + hh
                            for j in range(2):
                                E.tensor_tensor(
                                    ctxTh[(hg % 2) * 64:(hg % 2) * 64 + 64,
                                          hg // 2, ts(j, 512)],
                                    psBC[hh][0:64, ts(j, 512)],
                                    bcf[0:64, ts(2 * hh + j, 512)],
                                    ALU.mult)
                        E.last_inc(S("c_v"), 1)
                    p.on("v", ve)
                    prev_ve = v1
                core_done = p.val("c_t")
                wov = p.inc("d_wA", 16)
                if qh == 0:
                    gbv = p.inc("d_gb", 32)
                def gwo(E, cd=core_done, qh=qh):
                    E.wait_ge(S("c_t"), cd)
                    E.dma_start(out=wof[:, :, :],
                                in_=d["wo"].rearrange("(a p) f -> p a f", p=128)
                                ).then_inc(S("d_wA"), 16)
                    if qh == 0:
                        E.dma_start(out=gb_sb[:, 0:1, :],
                                    in_=d["gamma_bc"][:, None, :]).then_inc(S("d_gb"), 16)
                        E.dma_start(out=gb_sb[:, 1:2, :],
                                    in_=d["beta_bc"][:, None, :]).then_inc(S("d_gb"), 16)
                p.on("g", gwo)
                ln_sig = [0] * 8
                pv_sig = [0] * 8
                ps_free = [0] * 8
                s4_sig = [0] * 8
                v3_sig = [0] * 8
                for i in range(8):
                    mt = qh * 8 + i
                    par = i % 2
                    sname = "d_lx" + str(par)
                    lx = p.inc(sname, 16)
                    def gx(E, mt=mt, sname=sname,
                           guard=(ln_sig[i - 1] if i >= 1 else 0)):
                        if guard:
                            E.wait_ge(S("c_s"), guard)
                        E.dma_start(out=xch3[:, :],
                                    in_=d["x"].rearrange("(n p) e -> p n e", p=128)[:, mt, :]
                                    ).then_inc(S(sname), 16)
                    p.on("g", gx)
                    t0 = p.inc("c_t", 1)
                    def t(E, i=i, par=par, wov=wov, pve=prev_ve,
                          guard=(ps_free[i - 2] if i >= 2 else 0)):
                        if i == 0:
                            E.wait_ge(S("d_wA"), wov)
                            E.wait_ge(S("c_v"), pve)
                        if guard:
                            E.wait_ge(S("c_v"), guard)
                        ps = psBC[par]
                        for hc in range(DCH):
                            for j in range(2):
                                E.matmul(ps[:, ts(j, 512)], ctxTh[:, hc, ts(i)],
                                         wof[:, hc, ts(j, 512)],
                                         start=(hc == 0), stop=False)
                        for j in range(2):
                            E.matmul(ps[:, ts(j, 512)], onesf[0:1, 0:128],
                                     bvf[0:1, 1, ts(j, 512)], start=False, stop=True)
                        E.last_inc(S("c_t"), 1)
                    p.on("t", t)
                    s1 = p.inc("c_s", 1)
                    def sf(E, mt=mt, lx=lx, sname=sname,
                           guard=(pv_sig[i - 1] if i >= 1 else 0)):
                        E.wait_ge(S(sname), lx)
                        if guard:
                            E.wait_ge(S("c_v"), guard)
                        E.activation(xnch3[:, :], xch3[:, :], AF.Identity,
                                     bias=st0[:, mt, 1:2], scale=st0[:, mt, 0:1])
                        E.last_inc(S("c_s"), 1)
                    p.on("s", sf)
                    ln_sig[i] = s1
                    v0 = p.inc("c_v", 1)
                    def vf(E, par=par, s1=s1, t0=t0,
                           guard=(s4_sig[i - 1] if i >= 1 else 0)):
                        E.wait_ge(S("c_s"), s1)
                        E.wait_ge(S("c_t"), t0)
                        if guard:
                            E.wait_ge(S("c_s"), guard)
                        E.tensor_add(ych3[:, :], xnch3[:, :], psBC[par][:, :])
                        E.drain()
                        E.reduce_sum(tiny[:, 0:1], ych3[:, :], axis=AX)
                        E.last_inc(S("c_v"), 1)
                    p.on("v", vf)
                    pv_sig[i] = v0
                    ps_free[i] = v0
                    s2 = p.inc("c_s", 1)
                    def sf2(E, v0=v0):
                        E.wait_ge(S("c_v"), v0)
                        E.activation(sqd[:, 0, :], ych3[:, :], AF.Square,
                                     accum_out=tiny[:, 1:2])
                        E.last_inc(S("c_s"), 1)
                    p.on("s", sf2)
                    v1 = p.inc("c_v", 1)
                    def vf2(E, s2=s2):
                        E.wait_ge(S("c_s"), s2)
                        E.tensor_scalar_mul(tiny[:, 2:3], tiny[:, 0:1], 1.0 / D)
                        E.tensor_scalar_mul(tiny[:, 3:4], tiny[:, 1:2], 1.0 / D)
                        E.drain()
                        E.tensor_tensor(tiny[:, 4:5], tiny[:, 2:3], tiny[:, 2:3],
                                        ALU.mult)
                        E.drain()
                        E.tensor_sub(tiny[:, 4:5], tiny[:, 3:4], tiny[:, 4:5])
                        E.drain()
                        E.tensor_scalar_add(tiny[:, 4:5], tiny[:, 4:5], EPS2)
                        E.drain()
                        E.reciprocal(tiny[:, 4:5], tiny[:, 4:5])
                        E.tensor_scalar_mul(tiny[:, 5:6], tiny[:, 2:3], -1.0)
                        E.last_inc(S("c_v"), 1)
                    p.on("v", vf2)
                    s3 = p.inc("c_s", 1)
                    def sf3(E, v1=v1):
                        E.wait_ge(S("c_v"), v1)
                        E.activation(tiny[:, 6:7], tiny[:, 4:5], AF.Sqrt)
                        E.last_inc(S("c_s"), 1)
                    p.on("s", sf3)
                    v2 = p.inc("c_v", 1)
                    def vf3(E, s3=s3):
                        E.wait_ge(S("c_s"), s3)
                        E.tensor_tensor(tiny[:, 7:8], tiny[:, 5:6], tiny[:, 6:7],
                                        ALU.mult)
                        E.last_inc(S("c_v"), 1)
                    p.on("v", vf3)
                    s4 = p.inc("c_s", 1)
                    def sf4(E, v2=v2,
                            guard=(v3_sig[i - 1] if i >= 1 else 0)):
                        E.wait_ge(S("c_v"), v2)
                        if guard:
                            E.wait_ge(S("c_v"), guard)
                        E.activation(zch3[:, :], ych3[:, :], AF.Identity,
                                     bias=tiny[:, 7:8], scale=tiny[:, 6:7])
                        E.last_inc(S("c_s"), 1)
                    p.on("s", sf4)
                    s4_sig[i] = s4
                    v3 = p.inc("c_v", 1)
                    def vf4(E, par=par, s4=s4, gbv=gbv,
                            guard=(st_sig[mt - 2] if mt >= 2 else 0)):
                        E.wait_ge(S("c_s"), s4)
                        E.wait_ge(S("d_gb"), gbv)
                        if guard:
                            E.wait_ge(S("d_s" + str(par)), guard)
                        E.tensor_tensor(sch3[par][:, :], zch3[:, :],
                                        gb_sb[:, 0, :], ALU.mult)
                        E.drain()
                        E.tensor_add(sch3[par][:, :], sch3[par][:, :], gb_sb[:, 1, :])
                        E.last_inc(S("c_v"), 1)
                    p.on("v", vf4)
                    v3_sig[i] = v3
                    svv = p.inc("d_s" + str(par), 16)
                    def gst(E, mt=mt, par=par, v3=v3):
                        E.wait_ge(S("c_v"), v3)
                        E.dma_start(out=d["s_hbm"][ts(mt), :],
                                    in_=sch3[par][:, :]).then_inc(S("d_s" + str(par)), 16)
                    p.on("y", gst)
                    st_sig[mt] = svv
                outproj_done = p.val("c_t")
            if "s" in TAPS:
                a, b = p.val("d_s0"), p.val("d_s1")
                p.inc("d_tap", 16)
                def gt(E, a=a, b=b):
                    E.wait_ge(S("d_s0"), a)
                    E.wait_ge(S("d_s1"), b)
                    E.dma_start(out=d["tap_s"][:, :],
                                in_=d["s_hbm"][:, :]).then_inc(S("d_tap"), 16)
                p.on("g", gt)

        def attention(lay):
            wqn, wkn, wvn, won = (("wq", "wk", "wv", "wo") if lay == 0
                                  else ("wq1", "wk1", "wv1", "wout"))
            qb, kb = (0, 8) if lay == 0 else (16, 24)
            vrow, orow = 0, 1
            s_spill = d["s_hbm"] if lay == 0 else d["s1_hbm"]
            ssem = "d_s" if lay == 0 else "d_s1"

            p.inc("c_g", 1)
            def gi(E):
                E.memset(v_att[:, :, :, 64:66], 0.0)
                E.drain()
                E.memset(v_att[:, :, :, 64:65], 1.0)
                E.last_inc(S("c_g"), 1)
            p.on("g", gi)
            gready = p.val("c_g")
            cready = p.val("d_const")

            prev_ve = 0
            for grp in range(2):
                grp_tguard = p.val("c_t")
                wav = p.inc("d_wA", 32)
                def gw(E, grp=grp, gg=grp_tguard):
                    if gg:
                        E.wait_ge(S("c_t"), gg)
                    E.dma_start(out=wA[:, :, 0:512],
                                in_=d[wqn].rearrange("(a p) f -> p a f", p=128)[:, :, ts(grp, 512)]
                                ).then_inc(S("d_wA"), 16)
                    E.dma_start(out=wA[:, :, 512:1024],
                                in_=d[wkn].rearrange("(a p) f -> p a f", p=128)[:, :, ts(grp, 512)]
                                ).then_inc(S("d_wA"), 16)
                p.on("g", gw)
                wbv = p.inc("d_wB", 16)
                def gw2(E, grp=grp, gg=grp_tguard):
                    if gg:
                        E.wait_ge(S("c_t"), gg)
                    E.dma_start(out=wB[:, :, 0:512],
                                in_=d[wvn].rearrange("(a p) f -> p a f", p=128)[:, :, ts(grp, 512)]
                                ).then_inc(S("d_wB"), 16)
                p.on("g", gw2)

                for which in ("q", "k"):
                    dst = qTg if which == "q" else kTg
                    colbase = qb if which == "q" else kb
                    off = 0 if which == "q" else 512
                    for ht in range(4):
                        pe = p.val("c_s")
                        t0 = p.inc("c_t", 1)
                        def t(E, ht=ht, off=off, wav=wav, pe=pe,
                              first=(which == "q" and ht == 0),
                              pve=(prev_ve if (which == "q" and ht == 0) else 0)):
                            if first:
                                E.wait_ge(S("d_wA"), wav)
                            if pve:
                                E.wait_ge(S("c_v"), pve)
                            if pe:
                                E.wait_ge(S("c_s"), pe)
                            for dc in range(DCH):
                                for j in range(4):
                                    E.matmul(psA[:, ts(j, 512)],
                                             wA[:, dc, off + ht * 128:off + (ht + 1) * 128],
                                             xnT[:, dc, ts(j, 512)],
                                             start=(dc == 0), stop=(dc == DCH - 1))
                            E.last_inc(S("c_t"), 1)
                        p.on("t", t)
                        col = colbase + grp * 4 + ht
                        p.inc("c_s", 1)
                        def sf(E, ht=ht, dst=dst, col=col, t0=t0,
                               cready=(cready if (which == "q" and ht == 0 and grp == 0) else 0)):
                            E.wait_ge(S("c_t"), t0)
                            if cready:
                                E.wait_ge(S("d_const"), cready)
                            E.activation(dst[:, ht, :], psA[:, :], AF.Identity,
                                         bias=bqkc[:, col:col + 1])
                            E.last_inc(S("c_s"), 1)
                        p.on("s", sf)

                vev = [0] * NCH
                for mt in range(NCH):
                    par = mt % 2
                    t0 = p.inc("c_t", 1)
                    def t(E, mt=mt, par=par, wbv=wbv, grp=grp,
                          guard=(vev[mt - 2] if mt >= 2 else 0),
                          gready=gready, cready=cready):
                        if mt == 0:
                            E.wait_ge(S("d_wB"), wbv)
                            E.wait_ge(S("c_g"), gready)
                            E.wait_ge(S("d_const"), cready)
                        if guard:
                            E.wait_ge(S("c_s"), guard)
                        ps = psBC[par]
                        for dc in range(DCH):
                            E.matmul(ps[:, 0:512], xnT[:, dc, ts(mt)],
                                     wB[:, dc, 0:512], start=(dc == 0), stop=False)
                        E.matmul(ps[:, 0:512], ones[0:1, 0:128],
                                 brow[0:1, vrow, ts(grp, 512)],
                                 start=False, stop=True)
                        E.last_inc(S("c_t"), 1)
                    p.on("t", t)
                    s1 = p.inc("c_s", 1)
                    def sf(E, mt=mt, par=par, grp=grp, t0=t0):
                        E.wait_ge(S("c_t"), t0)
                        E.activation(v_att[:, mt, grp * 8:(grp + 1) * 8, 0:64],
                                     psBC[par][:, 0:512], AF.Identity)
                        E.last_inc(S("c_s"), 1)
                    p.on("s", sf)
                    vev[mt] = s1

                for pr in range(4):
                    he = grp * 8 + 2 * pr
                    for qh in range(2):
                        exp_sig = [0] * NCH
                        for scn in range(NCH):
                            par = scn % 2
                            t0 = p.inc("c_t", 1)
                            def t(E, scn=scn, pr=pr, qh=qh,
                                  guard=(exp_sig[scn - 1] if scn >= 1 else 0),
                                  pve=prev_ve):
                                if scn == 0 and pve:
                                    E.wait_ge(S("c_v"), pve)
                                if guard:
                                    E.wait_ge(S("c_s"), guard)
                                for j in range(2):
                                    q0 = qh * 1024 + j * 512
                                    E.matmul(psA[:, ts(j, 512)],
                                             kTg[0:64, pr, ts(scn)],
                                             qTg[0:64, pr, q0:q0 + 512],
                                             start=True, stop=True,
                                             tile_position=(0, 0))
                                    E.matmul(psA[:, 1024 + j * 512:1536 + j * 512],
                                             kTg[64:128, pr, ts(scn)],
                                             qTg[64:128, pr, q0:q0 + 512],
                                             start=True, stop=True,
                                             tile_position=(64, 0))
                                E.last_inc(S("c_t"), 1)
                            p.on("t", t)
                            s1 = p.inc("c_s", 1)
                            def sf(E, par=par, t0=t0):
                                E.wait_ge(S("c_t"), t0)
                                E.activation(attnT[par][:, :], psA[:, :],
                                             AF.Exp, scale=0.125)
                                E.last_inc(S("c_s"), 1)
                            p.on("s", sf)
                            exp_sig[scn] = s1
                            p.inc("c_t", 1)
                            def tc(E, scn=scn, par=par, he=he, s1=s1, pve=prev_ve):
                                E.wait_ge(S("c_s"), s1)
                                if scn == 0 and pve:
                                    E.wait_ge(S("c_v"), pve)
                                first, last = scn == 0, scn == NCH - 1
                                for hh in range(2):
                                    for j in range(2):
                                        E.matmul(psBC[hh][0:65, ts(j, 512)],
                                                 v_att[:, scn, he + hh, 0:65],
                                                 attnT[par][:, hh * 1024 + j * 512:
                                                            hh * 1024 + (j + 1) * 512],
                                                 start=first, stop=last)
                                E.last_inc(S("c_t"), 1)
                            p.on("t", tc)
                        tctx = p.val("c_t")
                        v0 = p.inc("c_v", 1)
                        def vd(E, tctx=tctx):
                            E.wait_ge(S("c_t"), tctx)
                            E.tensor_copy(rowsc[0:1, 0:2, :],
                                          psB[64:65, :].rearrange("p (a b) -> p a b", a=2))
                            E.tensor_copy(rowsc[0:1, 2:4, :],
                                          psC[64:65, :].rearrange("p (a b) -> p a b", a=2))
                            E.drain()
                            E.reciprocal(rowsc[0:1, :, :], rowsc[0:1, :, :])
                            E.last_inc(S("c_v"), 1)
                        p.on("v", vd)
                        t2 = p.inc("c_t", 1)
                        def tb(E, v0=v0):
                            E.wait_ge(S("c_v"), v0)
                            for j in range(4):
                                E.matmul(psA[0:64, ts(j, 512)], onesf[0:1, 0:64],
                                         rowsc[0:1, j, :], start=True, stop=True)
                            E.last_inc(S("c_t"), 1)
                        p.on("t", tb)
                        v1 = p.inc("c_v", 1)
                        def ve(E, qh=qh, he=he, t2=t2):
                            E.wait_ge(S("c_t"), t2)
                            E.tensor_copy(bcf1[0:64, :], psA[0:64, :])
                            E.drain()
                            for hh in range(2):
                                hg = he + hh
                                for j in range(2):
                                    q0 = qh * 1024 + j * 512
                                    E.tensor_tensor(
                                        ctxT[(hg % 2) * 64:(hg % 2) * 64 + 64,
                                             hg // 2, q0:q0 + 512],
                                        psBC[hh][0:64, ts(j, 512)],
                                        bcf1[0:64, ts(2 * hh + j, 512)],
                                        ALU.mult)
                            E.last_inc(S("c_v"), 1)
                        p.on("v", ve)
                        prev_ve = v1

            t_all = p.val("c_t")
            wov = p.inc("d_wA", 16)
            def gwo(E, wov=wov, tg=t_all):
                E.wait_ge(S("c_t"), tg)
                E.dma_start(out=wA[:, :, :],
                            in_=d[won].rearrange("(a p) f -> p a f", p=128)
                            ).then_inc(S("d_wA"), 16)
            p.on("g", gwo)
            ps_free = [0] * NCH
            buf_free = [0] * NCH
            st_sig = [0] * NCH
            lvs = [0] * NCH
            for mt in range(NCH):
                par = mt % 2
                if lay == 0:
                    sname = "d_ld" + str(par)
                    lvs[mt] = p.inc(sname, 16)
                    def gx(E, mt=mt, par=par, sname=sname,
                           guard=(buf_free[mt - 2] if mt >= 2 else 0)):
                        if guard:
                            E.wait_ge(S("c_s"), guard)
                        E.dma_start(out=xch[par][:, :],
                                    in_=d["x"].rearrange("(n p) e -> p n e", p=128)[:, mt, :]
                                    ).then_inc(S(sname), 16)
                    p.on("g", gx)
                t0 = p.inc("c_t", 1)
                def t(E, mt=mt, wov=wov, pve=prev_ve, orow=orow, lay=lay,
                      guard=(ps_free[mt - 2] if mt >= 2 else 0)):
                    if mt == 0:
                        E.wait_ge(S("d_wA"), wov)
                        E.wait_ge(S("c_v"), pve)
                    if guard:
                        E.wait_ge(S("c_v" if lay == 0 else "c_s"), guard)
                    ps = psBC[mt % 2]
                    for hc in range(DCH):
                        for j in range(2):
                            E.matmul(ps[:, ts(j, 512)], ctxT[:, hc, ts(mt)],
                                     wA[:, hc, ts(j, 512)],
                                     start=(hc == 0), stop=False)
                    for j in range(2):
                        E.matmul(ps[:, ts(j, 512)], ones[0:1, 0:128],
                                 brow[0:1, orow, ts(j, 512)],
                                 start=False, stop=True)
                    E.last_inc(S("c_t"), 1)
                p.on("t", t)
                if lay == 0:
                    sname = "d_ld" + str(par)
                    s1 = p.inc("c_s", 1)
                    def sf(E, mt=mt, par=par, lv=lvs[mt], sname=sname,
                           guard=(ps_free[mt - 2] if mt >= 2 else 0)):
                        E.wait_ge(S(sname), lv)
                        if guard:
                            E.wait_ge(S("c_v"), guard)
                        E.activation(xnf2[par][:, :], xch[par][:, :], AF.Identity,
                                     bias=st0[:, mt, 1:2], scale=st0[:, mt, 0:1])
                        E.last_inc(S("c_s"), 1)
                    p.on("s", sf)
                    v0 = p.inc("c_v", 1)
                    def vf(E, mt=mt, par=par, s1=s1, t0=t0):
                        E.wait_ge(S("c_s"), s1)
                        E.wait_ge(S("c_t"), t0)
                        E.tensor_add(ych[par][:, :], xnf2[par][:, :], psBC[par][:, :])
                        E.drain()
                        E.reduce_sum(tiny[:, 8 * par:8 * par + 1], ych[par][:, :], axis=AX)
                        E.last_inc(S("c_v"), 1)
                    p.on("v", vf)
                    ps_free[mt] = v0
                    s2 = p.inc("c_s", 1)
                    def sf2(E, par=par, v0=v0):
                        E.wait_ge(S("c_v"), v0)
                        E.activation(sqd[:, par, :], ych[par][:, :], AF.Square,
                                     accum_out=tiny[:, 8 * par + 1:8 * par + 2])
                        E.last_inc(S("c_s"), 1)
                    p.on("s", sf2)
                    v1 = p.inc("c_v", 1)
                    def vf2(E, par=par, s2=s2):
                        b = 8 * par
                        E.wait_ge(S("c_s"), s2)
                        E.tensor_scalar_mul(tiny[:, b + 2:b + 3], tiny[:, b:b + 1], 1.0 / D)
                        E.tensor_scalar_mul(tiny[:, b + 3:b + 4], tiny[:, b + 1:b + 2], 1.0 / D)
                        E.drain()
                        E.tensor_tensor(tiny[:, b + 4:b + 5], tiny[:, b + 2:b + 3],
                                        tiny[:, b + 2:b + 3], ALU.mult)
                        E.drain()
                        E.tensor_sub(tiny[:, b + 4:b + 5], tiny[:, b + 3:b + 4],
                                     tiny[:, b + 4:b + 5])
                        E.drain()
                        E.tensor_scalar_add(tiny[:, b + 4:b + 5], tiny[:, b + 4:b + 5], EPS2)
                        E.drain()
                        E.reciprocal(tiny[:, b + 4:b + 5], tiny[:, b + 4:b + 5])
                        E.tensor_scalar_mul(tiny[:, b + 5:b + 6], tiny[:, b + 2:b + 3], -1.0)
                        E.last_inc(S("c_v"), 1)
                    p.on("v", vf2)
                    s3 = p.inc("c_s", 1)
                    def sf3(E, par=par, v1=v1):
                        b = 8 * par
                        E.wait_ge(S("c_v"), v1)
                        E.activation(tiny[:, b + 6:b + 7], tiny[:, b + 4:b + 5], AF.Sqrt)
                        E.last_inc(S("c_s"), 1)
                    p.on("s", sf3)
                    v2 = p.inc("c_v", 1)
                    def vf3(E, par=par, s3=s3):
                        b = 8 * par
                        E.wait_ge(S("c_s"), s3)
                        E.tensor_tensor(tiny[:, b + 7:b + 8], tiny[:, b + 5:b + 6],
                                        tiny[:, b + 6:b + 7], ALU.mult)
                        E.last_inc(S("c_v"), 1)
                    p.on("v", vf3)
                    s4 = p.inc("c_s", 1)
                    def sf4(E, par=par, v2=v2):
                        b = 8 * par
                        E.wait_ge(S("c_v"), v2)
                        E.activation(zch[par][:, :], ych[par][:, :], AF.Identity,
                                     bias=tiny[:, b + 7:b + 8], scale=tiny[:, b + 6:b + 7])
                        E.last_inc(S("c_s"), 1)
                    p.on("s", sf4)
                    buf_free[mt] = s4
                    v3 = p.inc("c_v", 1)
                    def vf4(E, mt=mt, par=par, s4=s4, ssem=ssem,
                            guard=(st_sig[mt - 2] if mt >= 2 else 0),
                            cready=(cready if mt == 0 else 0)):
                        E.wait_ge(S("c_s"), s4)
                        if cready:
                            E.wait_ge(S("d_const"), cready)
                        if guard:
                            E.wait_ge(S(ssem + str(par)), guard)
                        E.tensor_tensor(sch[par][:, :], zch[par][:, :],
                                        gb_sb[:, 0, :], ALU.mult)
                        E.drain()
                        E.tensor_add(sch[par][:, :], sch[par][:, :], gb_sb[:, 1, :])
                        E.last_inc(S("c_v"), 1)
                    p.on("v", vf4)
                    sv = p.inc(ssem + str(par), 16)
                    def gst(E, mt=mt, par=par, v3=v3, ssem=ssem):
                        E.wait_ge(S("c_v"), v3)
                        E.dma_start(out=s_spill[ts(mt), :],
                                    in_=sch[par][:, :]).then_inc(S(ssem + str(par)), 16)
                    p.on("y", gst)
                    st_sig[mt] = sv
                else:
                    s1 = p.inc("c_s", 1)
                    def sf(E, mt=mt, par=par, t0=t0, ssem=ssem,
                           guard=(st_sig[mt - 2] if mt >= 2 else 0)):
                        E.wait_ge(S("c_t"), t0)
                        if guard:
                            E.wait_ge(S(ssem + str(par)), guard)
                        E.activation(sch[par][:, :], psBC[par][:, :], AF.Identity)
                        E.last_inc(S("c_s"), 1)
                    p.on("s", sf)
                    ps_free[mt] = s1
                    sv = p.inc(ssem + str(par), 16)
                    def gst(E, mt=mt, par=par, s1=s1, ssem=ssem):
                        E.wait_ge(S("c_s"), s1)
                        E.dma_start(out=s_spill[ts(mt), :],
                                    in_=sch[par][:, :]).then_inc(S(ssem + str(par)), 16)
                    p.on("y", gst)
                    st_sig[mt] = sv
            if ("s" in TAPS and lay == 0) or ("s1" in TAPS and lay == 1):
                a, b = p.val(ssem + "0"), p.val(ssem + "1")
                p.inc("d_tap", 16)
                def gt(E, a=a, b=b, lay=lay, ssem=ssem):
                    E.wait_ge(S(ssem + "0"), a)
                    E.wait_ge(S(ssem + "1"), b)
                    E.dma_start(out=d["tap_s" if lay == 0 else "tap_s1"][:, :],
                                in_=s_spill[:, :]).then_inc(S("d_tap"), 16)
                p.on("g", gt)

        def p4():
            ccv = p.val("cc")
            ap_sig = [0] * NCH
            for c in range(NCH):
                par = c % 2
                sname = "d_ld" + str(par)
                lv = p.inc(sname, 16)
                def gld(E, c=c, par=par, sname=sname,
                        sw=((p.val("d_s0"), p.val("d_s1")) if c < 2 else None),
                        guard=(ap_sig[c - 2] if c >= 2 else 0)):
                    if sw:
                        E.wait_ge(S("d_s0"), sw[0])
                        E.wait_ge(S("d_s1"), sw[1])
                    if guard:
                        E.wait_ge(S("c_s"), guard)
                    E.dma_start(out=Ezb[par][:, :],
                                in_=d["x"].rearrange("(n p) e -> p n e", p=128)[:, c, :]
                                ).then_inc(S(sname), 16)
                p.on("g", gld)
                s1 = p.inc("c_s", 1)
                def sf(E, c=c, par=par, lv=lv, sname=sname):
                    E.wait_ge(S(sname), lv)
                    E.activation(xn_tok[:, c, :], Ezb[par][:, :], AF.Identity,
                                 bias=st0[:, c, 1:2], scale=st0[:, c, 0:1])
                    E.last_inc(S("c_s"), 1)
                p.on("s", sf)
                ap_sig[c] = s1
            xn_tok_done = p.val("c_s")

            dv_sig = [0] * (2 * NCH)
            half_done = [0, 0]
            sev_st = [0] * (2 * DCH)
            slot_ev = [0] * (2 * DCH)
            for nh in range(2):
                for c in range(NCH):
                    gi_ = nh * NCH + c
                    par = c % 2
                    sname = "d_ez" + str(par)
                    lv = p.inc(sname, 32)
                    def gld(E, c=c, nh=nh, par=par, sname=sname, ccv=ccv,
                            xd=(xn_tok_done if (nh == 0 and c < 2) else 0),
                            guard=(dv_sig[gi_ - 2] if gi_ >= 2 else 0)):
                        if c == 0 and nh == 0:
                            E.wait_ge(S("cc"), ccv)
                        if xd:
                            E.wait_ge(S("c_s"), xd)
                        if guard:
                            E.wait_ge(S("c_v"), guard)
                        E.dma_start(out=Ezb[par][:, :],
                                    in_=d["E_hbm"][ts(c), ts(nh, 1024)]
                                    ).then_inc(S(sname), 16)
                        E.dma_start(out=Zzb[par][:, :],
                                    in_=d["Z_hbm"][ts(c), ts(nh, 1024)]
                                    ).then_inc(S(sname), 16)
                    p.on("g", gld)
                    v0 = p.inc("c_v", 1)
                    def vf(E, c=c, par=par, lv=lv, sname=sname,
                           hw=(half_done[0] if (nh == 1 and c < 2) else 0),
                           sw=((p.val("d_s0"), p.val("d_s1"))
                               if (c == 0 and nh == 0) else None)):
                        if sw:
                            E.wait_ge(S("d_s0"), sw[0])
                            E.wait_ge(S("d_s1"), sw[1])
                        if hw:
                            E.wait_ge(S("c_s"), hw)
                        E.wait_ge(S(sname), lv)
                        E.reciprocal(Zzb[par][:, :], Zzb[par][:, :])
                        E.drain()
                        E.tensor_tensor(disp[:, c, :], Ezb[par][:, :],
                                        Zzb[par][:, :], ALU.mult)
                        E.last_inc(S("c_v"), 1)
                    p.on("v", vf)
                    dv_sig[gi_] = v0
                disp_done = p.val("c_v")
                for dc in range(DCH):
                    gi2 = nh * DCH + dc
                    pssel = gi2 % 2
                    prevps = slot_ev[gi2 - 2] if gi2 >= 2 else 0
                    t0 = p.inc("c_t", 1)
                    def tf(E, dc=dc, pssel=pssel, dd=disp_done, prevps=prevps,
                           xd=xn_tok_done):
                        E.wait_ge(S("c_v"), dd)
                        E.wait_ge(S("c_s"), xd)
                        if prevps:
                            E.wait_ge(S("c_s"), prevps)
                        ps = psBC[pssel]
                        for c in range(NCH):
                            for j in range(2):
                                E.matmul(ps[:, ts(j, 512)], xn_tok[:, c, ts(dc)],
                                         disp[:, c, ts(j, 512)],
                                         start=(c == 0), stop=(c == NCH - 1))
                        E.last_inc(S("c_t"), 1)
                    p.on("t", tf)
                    s1 = p.inc("c_s", 1)
                    def sf(E, pssel=pssel, t0=t0,
                           guard=(sev_st[gi2 - 2] if gi2 >= 2 else 0)):
                        E.wait_ge(S("c_t"), t0)
                        if guard:
                            E.wait_ge(S("d_slt" + str(pssel)), guard)
                        E.activation(sev[pssel][:, :], psBC[pssel][:, :],
                                     AF.Identity)
                        E.last_inc(S("c_s"), 1)
                    p.on("s", sf)
                    slot_ev[gi2] = s1
                    stv = p.inc("d_slt" + str(pssel), 16)
                    def yst(E, dc=dc, nh=nh, pssel=pssel, s1=s1):
                        E.wait_ge(S("c_s"), s1)
                        E.dma_start(out=d["slotsT_hbm"][ts(dc), ts(nh, 1024)],
                                    in_=sev[pssel][:, :]
                                    ).then_inc(S("d_slt" + str(pssel)), 16)
                    p.on("y", yst)
                    sev_st[gi2] = stv
                half_done[nh] = p.val("c_s")
            slots_done = p.val("c_s")
            sltv = (p.val("d_slt0"), p.val("d_slt1"))
            if "slotsT" in TAPS:
                p.inc("d_tap", 16)
                def gt(E, sltv=sltv):
                    E.wait_ge(S("d_slt0"), sltv[0])
                    E.wait_ge(S("d_slt1"), sltv[1])
                    E.dma_start(out=d["tap_slotsT"][:, :],
                                in_=d["slotsT_hbm"][:, :]).then_inc(S("d_tap"), 16)
                p.on("g", gt)

            yq_ev = [0] * 16
            wsig = [0] * 64
            bq_lv = [0] * 16
            for n in range(64):
                par = n % 2
                qd, qpos = n // 4, n % 4
                sname = "d_wA" if par == 0 else "d_wB"
                wv = p.inc(sname, 16)
                def gw(E, n=n, par=par, sname=sname, sd=slots_done,
                       guard=(wsig[n - 2] if n >= 2 else 0)):
                    if n < 2:
                        E.wait_ge(S("c_s"), sd)
                    if guard:
                        E.wait_ge(S("c_t"), guard)
                    E.dma_start(out=w0f[par][:, :, :],
                                in_=d["w0"][n, :, :].rearrange("(a p) e -> p a e", p=128)
                                ).then_inc(S(sname), 16)
                p.on("g", gw)
                scv = p.inc("d_sc" + str(par), 16)
                def gsc(E, n=n, par=par, sltv=sltv,
                        guard=(wsig[n - 2] if n >= 2 else 0)):
                    if n < 2:
                        E.wait_ge(S("d_slt0"), sltv[0])
                        E.wait_ge(S("d_slt1"), sltv[1])
                    if guard:
                        E.wait_ge(S("c_t"), guard)
                    E.dma_start(
                        out=scol[par][:, :, :],
                        in_=d["slotsT_hbm"].rearrange("(a p) n -> p a n", p=128)
                        [:, :, n * 32:(n + 1) * 32]
                    ).then_inc(S("d_sc" + str(par)), 16)
                p.on("g", gsc)
                if qpos == 0:
                    bq_lv[qd] = p.inc("d_b0e" + str(qd % 2), 16)
                    def gbq(E, qd=qd, sd=slots_done,
                            guard=(yq_ev[qd - 2] if qd >= 2 else 0)):
                        if qd < 2:
                            E.wait_ge(S("c_s"), sd)
                        if guard:
                            E.wait_ge(S("c_v"), guard)
                        E.dma_start(out=b0q[qd % 2][:, :],
                                    in_=d["b0e"].rearrange("(n p) e -> p n e", p=128)[:, qd, :]
                                    ).then_inc(S("d_b0e" + str(qd % 2)), 16)
                    p.on("g", gbq)
                t0 = p.inc("c_t", 1)
                def tf(E, par=par, qd=qd, qpos=qpos, wv=wv, scv=scv, sname=sname,
                       guard=(yq_ev[qd - 2] if (qd >= 2 and qpos == 0) else 0)):
                    E.wait_ge(S(sname), wv)
                    E.wait_ge(S("d_sc" + str(par)), scv)
                    if guard:
                        E.wait_ge(S("c_v"), guard)
                    ps = psBC[qd % 2]
                    for dc in range(DCH):
                        for j in range(2):
                            E.matmul(ps[32 * qpos:32 * qpos + 32, ts(j, 512)],
                                     scol[par][:, dc, :],
                                     w0f[par][:, dc, ts(j, 512)],
                                     start=(dc == 0), stop=(dc == DCH - 1),
                                     tile_position=(0, 32 * qpos))
                    E.last_inc(S("c_t"), 1)
                p.on("t", tf)
                wsig[n] = t0
                if qpos == 3:
                    v0 = p.inc("c_v", 1)
                    def vf(E, qd=qd, t0=t0, bqv=bq_lv[qd], sltv=sltv):
                        E.wait_ge(S("c_t"), t0)
                        E.wait_ge(S("d_b0e" + str(qd % 2)), bqv)
                        if qd >= 14:
                            E.wait_ge(S("d_slt0"), sltv[0])
                            E.wait_ge(S("d_slt1"), sltv[1])
                        E.tensor_add(y_f[:, qd, :], psBC[qd % 2][:, :],
                                     b0q[qd % 2][:, :])
                        E.last_inc(S("c_v"), 1)
                    p.on("v", vf)
                    yq_ev[qd] = v0
            y_done = p.val("c_v")
            if "y" in TAPS:
                p.inc("d_tap", 16)
                def gt(E, yd=y_done):
                    E.wait_ge(S("c_v"), yd)
                    E.dma_start(out=d["tap_y"].rearrange("(n p) e -> p n e", p=128),
                                in_=y_f[:, :, :]).then_inc(S("d_tap"), 16)
                p.on("g", gt)

            tr_sig = [0] * NCH
            g_sig = [0] * NCH
            u_sig = [0] * NCH
            xst = [0] * NCH
            for mt in range(NCH):
                par = mt % 2
                sname = "d_ez" + str(par)
                lv = p.inc(sname, 48)
                def gld(E, mt=mt, par=par, sname=sname, yd=y_done,
                        sw=((p.val("d_s0"), p.val("d_s1")) if mt < 2 else None),
                        tw=(p.val("d_tap") if (mt < 2 and TAPS) else 0),
                        guard=(u_sig[mt - 2] if mt >= 2 else 0)):
                    if mt < 2:
                        E.wait_ge(S("c_v"), yd)
                        E.wait_ge(S("d_s0"), sw[0])
                        E.wait_ge(S("d_s1"), sw[1])
                        if tw:
                            E.wait_ge(S("d_tap"), tw)
                    if guard:
                        E.wait_ge(S("c_v"), guard)
                    E.dma_start(out=Erl[par][:, :],
                                in_=d["E_hbm"][ts(mt), :]).then_inc(S(sname), 16)
                    E.dma_start(out=x4ch[par][:, :],
                                in_=d["x"].rearrange("(n p) e -> p n e", p=128)[:, mt, :]
                                ).then_inc(S(sname), 16)
                    E.dma_start(out=s4ch[par][:, :],
                                in_=d["s_hbm"].rearrange("(n p) e -> p n e", p=128)[:, mt, :]
                                ).then_inc(S(sname), 16)
                p.on("g", gld)
                t0 = p.inc("c_t", 1)
                def tf(E, mt=mt, par=par, lv=lv, sname=sname,
                       guard=(tr_sig[mt - 2] if mt >= 2 else 0)):
                    E.wait_ge(S(sname), lv)
                    if guard:
                        E.wait_ge(S("c_s"), guard)
                    for b in range(4):
                        for k in range(4):
                            E.matmul(psA[:, b * 512 + k * 128:b * 512 + (k + 1) * 128],
                                     Erl[par][:, ts(4 * b + k)], ident[:, :],
                                     start=True, stop=True, is_transpose=True)
                    E.last_inc(S("c_t"), 1)
                p.on("t", tf)
                s1 = p.inc("c_s", 1)
                def sf(E, par=par, t0=t0):
                    E.wait_ge(S("c_t"), t0)
                    E.activation(combt[par][:, :, :],
                                 psA.rearrange("p (a b) -> p a b", a=NCH),
                                 AF.Identity)
                    E.last_inc(S("c_s"), 1)
                p.on("s", sf)
                tr_sig[mt] = s1
                v0 = p.inc("c_v", 1)
                def vs(E, par=par, lv=lv, sname=sname):
                    E.wait_ge(S(sname), lv)
                    E.reduce_sum(tiny[:, 16 + 4 * par:17 + 4 * par],
                                 Erl[par][:, :], axis=AX)
                    E.drain()
                    E.reciprocal(tiny[:, 17 + 4 * par:18 + 4 * par],
                                 tiny[:, 16 + 4 * par:17 + 4 * par])
                    E.last_inc(S("c_v"), 1)
                p.on("v", vs)
                t1 = p.inc("c_t", 1)
                def tf2(E, par=par, s1=s1,
                        guard=(g_sig[mt - 2] if mt >= 2 else 0)):
                    E.wait_ge(S("c_s"), s1)
                    if guard:
                        E.wait_ge(S("c_s"), guard)
                    ps = psBC[par]
                    for ncc in range(NCH):
                        for j in range(2):
                            E.matmul(ps[:, ts(j, 512)], combt[par][:, ncc, :],
                                     y_f[:, ncc, ts(j, 512)],
                                     start=(ncc == 0), stop=(ncc == NCH - 1))
                    E.last_inc(S("c_t"), 1)
                p.on("t", tf2)
                s2 = p.inc("c_s", 1)
                def sg(E, par=par, t1=t1, v0=v0):
                    E.wait_ge(S("c_t"), t1)
                    E.wait_ge(S("c_v"), v0)
                    E.activation(gch[par][:, :], psBC[par][:, :], AF.Silu,
                                 scale=tiny[:, 17 + 4 * par:18 + 4 * par])
                    E.last_inc(S("c_s"), 1)
                p.on("s", sg)
                g_sig[mt] = s2
                v1 = p.inc("c_v", 1)
                def vu(E, mt=mt, par=par, s2=s2,
                       guard=(xst[mt - 2] if mt >= 2 else 0)):
                    E.wait_ge(S("c_s"), s2)
                    if guard:
                        E.wait_ge(S("d_x1" + str(par)), guard)
                    E.tensor_tensor(o4ch[par][:, :], s4ch[par][:, :],
                                    gch[par][:, :], ALU.mult)
                    E.drain()
                    E.tensor_add(o4ch[par][:, :], x4ch[par][:, :], o4ch[par][:, :])
                    E.drain()
                    E.reduce_sum(sac[:, mt, 0:1], o4ch[par][:, :], axis=AX)
                    E.last_inc(S("c_v"), 1)
                p.on("v", vu)
                u_sig[mt] = v1
                s3 = p.inc("c_s", 1)
                def sq(E, mt=mt, par=par, v1=v1):
                    E.wait_ge(S("c_v"), v1)
                    E.activation(sqd[:, par, :], o4ch[par][:, :], AF.Square,
                                 accum_out=sac[:, mt, 1:2])
                    E.last_inc(S("c_s"), 1)
                p.on("s", sq)
                xv = p.inc("d_x1" + str(par), 16)
                def gst(E, mt=mt, par=par, s3=s3):
                    E.wait_ge(S("c_s"), s3)
                    E.dma_start(out=d["x1_hbm"][ts(mt), :],
                                in_=o4ch[par][:, :]).then_inc(S("d_x1" + str(par)), 16)
                p.on("y", gst)
                xst[mt] = xv
            if "x1" in TAPS:
                a, b = p.val("d_x10"), p.val("d_x11")
                p.inc("d_tap", 16)
                def gt(E, a=a, b=b):
                    E.wait_ge(S("d_x10"), a)
                    E.wait_ge(S("d_x11"), b)
                    E.dma_start(out=d["tap_x1"][:, :],
                                in_=d["x1_hbm"][:, :]).then_inc(S("d_tap"), 16)
                p.on("g", gt)

        def p5():
            a, b = p.val("d_x10"), p.val("d_x11")
            p.inc("c_v", 1)
            def vw(E, a=a, b=b):
                E.wait_ge(S("d_x10"), a)
                E.wait_ge(S("d_x11"), b)
                E.last_inc(S("c_v"), 1)
            p.on("v", vw)
            stats_batched(st1, EPS1)
            rv = p.inc("d_rw", 16)
            def g(E, rv=rv):
                E.dma_start(out=rw_sb[:, :], in_=d["rw"][:, :]).then_inc(S("d_rw"), 16)
            p.on("g", g)
            t0 = p.inc("c_t", 1)
            def t(E, rv=rv):
                E.wait_ge(S("d_rw"), rv)
                for dc in range(DCH):
                    E.matmul(psB[0:128, dc * 8:dc * 8 + 8], rw_sb[:, ts(dc)],
                             ident[0:8, 0:8], start=True, stop=True,
                             is_transpose=True)
                E.last_inc(S("c_t"), 1)
            p.on("t", t)
            s0 = p.inc("c_s", 1)
            def s(E, t0=t0):
                E.wait_ge(S("c_t"), t0)
                E.activation(rwT[:, :, :],
                             psB[:, 0:64].rearrange("p (a b) -> p a b", a=DCH),
                             AF.Identity)
                E.last_inc(S("c_s"), 1)
            p.on("s", s)
            rwdone = p.val("c_s")
            p.inc("c_t", 1)
            def t2(E, s0=s0):
                E.wait_ge(S("c_s"), s0)
                E.last_inc(S("c_t"), 1)
            p.on("t", t2)
            ln_to_T(d["x1_hbm"], st1, xnT, x5ch, xn5, preloaded=None,
                    stats=False, tap=("xn1T" if "xn1T" in TAPS else None),
                    pre_waits=(("d_x10", p.val("d_x10")), ("d_x11", p.val("d_x11"))),
                    router=True)
            tall = p.val("c_t")
            p.inc("c_v", 1)
            def vre(E, tall=tall):
                E.wait_ge(S("c_t"), tall)
                E.tensor_copy(rlograw[:, :, :],
                              psA[:, 0:128].rearrange("p (a b) -> p a b", a=NCH))
                E.last_inc(S("c_v"), 1)
            p.on("v", vre)

        def p7():
            v0 = p.inc("c_v", 1)
            def v(E, cready=p.val("d_const"),
                  sw=(p.val("d_s10"), p.val("d_s11"))):
                E.wait_ge(S("d_const"), cready)
                E.wait_ge(S("d_s10"), sw[0])
                E.wait_ge(S("d_s11"), sw[1])
                E.tensor_tensor(rlog[:, :, :], rlograw[:, :, :],
                                ebias[:, None, :].broadcast_to([128, NCH, NE]),
                                ALU.add)
                E.drain()
                for mt in range(NCH):
                    E.reduce_max(tiny[:, mt:mt + 1], rlog[:, mt, :], axis=AX,
                                 negate=True)
                E.last_inc(S("c_v"), 1)
            p.on("v", v)
            s1 = p.inc("c_s", 1)
            def s2(E, v0=v0):
                E.wait_ge(S("c_v"), v0)
                for mt in range(NCH):
                    E.activation(rsm[:, mt, :], rlog[:, mt, :], AF.Exp,
                                 bias=tiny[:, mt:mt + 1],
                                 accum_out=tiny[:, 16 + mt:17 + mt])
                E.last_inc(S("c_s"), 1)
            p.on("s", s2)
            v1 = p.inc("c_v", 1)
            def v2(E, s1=s1):
                E.wait_ge(S("c_s"), s1)
                E.reciprocal(tiny[:, 32:48], tiny[:, 16:32])
                E.drain()
                for mt in range(NCH):
                    E.tensor_scalar_mul(rsm[:, mt, :], rsm[:, mt, :],
                                        tiny[:, 32 + mt:33 + mt])
                E.drain()
                for mt in range(NCH):
                    E.reduce_max(tiny[:, mt:mt + 1], rsm[:, mt, :], axis=AX)
                E.drain()
                E.memset(tiny[:, 48:64], 1.0)
                E.drain()
                for e in range(NE):
                    E.tensor_tensor(rs2[:, :, 0], rsm[:, :, e], tiny[:, 0:16],
                                    ALU.is_equal)
                    E.drain()
                    E.tensor_tensor(rlog[:, :, e], rs2[:, :, 0], tiny[:, 48:64],
                                    ALU.mult)
                    E.drain()
                    E.tensor_sub(tiny[:, 48:64], tiny[:, 48:64], rlog[:, :, e])
                    E.drain()
                E.scalar_tensor_tensor(rs3[:, :, :], rlog[:, :, :], -1e30,
                                       rsm[:, :, :], ALU.mult, ALU.add)
                E.drain()
                for mt in range(NCH):
                    E.reduce_max(tiny[:, 16 + mt:17 + mt], rs3[:, mt, :], axis=AX)
                E.drain()
                E.memset(tiny[:, 48:64], 1.0)
                E.drain()
                for e in range(NE):
                    E.tensor_tensor(rs2[:, :, 0], rs3[:, :, e], tiny[:, 16:32],
                                    ALU.is_equal)
                    E.drain()
                    E.tensor_tensor(rs4[:, :, e], rs2[:, :, 0], tiny[:, 48:64],
                                    ALU.mult)
                    E.drain()
                    E.tensor_sub(tiny[:, 48:64], tiny[:, 48:64], rs4[:, :, e])
                    E.drain()
                E.tensor_sub(tiny[:, 32:48], tiny[:, 0:16], tiny[:, 16:32])
                E.last_inc(S("c_v"), 1)
            p.on("v", v2)
            s2b = p.inc("c_s", 1)
            def s3(E, v1=v1):
                E.wait_ge(S("c_v"), v1)
                E.activation(tiny[:, 0:16], tiny[:, 32:48], AF.Sigmoid)
                E.last_inc(S("c_s"), 1)
            p.on("s", s3)
            v2b = p.inc("c_v", 1)
            def v3(E, s2b=s2b):
                E.wait_ge(S("c_s"), s2b)
                E.tensor_scalar(tiny[:, 16:32], tiny[:, 0:16], -1.0, 1.0,
                                ALU.mult, ALU.add)
                E.drain()
                E.tensor_tensor(comb[:, :, :], rlog[:, :, :],
                                tiny[:, 0:16][:, :, None].broadcast_to([128, NCH, NE]),
                                ALU.mult)
                E.drain()
                E.tensor_tensor(rs3[:, :, :], rs4[:, :, :],
                                tiny[:, 16:32][:, :, None].broadcast_to([128, NCH, NE]),
                                ALU.mult)
                E.drain()
                E.tensor_add(comb[:, :, :], comb[:, :, :], rs3[:, :, :])
                E.last_inc(S("c_v"), 1)
            p.on("v", v3)
            if "comb" in TAPS:
                p.inc("d_tap", 16)
                def gt(E, v2b=v2b):
                    E.wait_ge(S("c_v"), v2b)
                    E.dma_start(out=d["tap_comb"].rearrange("(n p) e -> p n e", p=128),
                                in_=comb[:, :, :]).then_inc(S("d_tap"), 16)
                p.on("g", gt)

        def p8():
            combv = p.val("c_v")
            castv = p.val("d_castA")
            q_cons = []
            d_cons = []
            oacc_sig = [[0] * NCH for _ in range(19)]
            gbufs = [wA[:, :, 0:512], wA[:, :, 512:1024]]
            ubufs = [wB[:, :, 0:512], wB[:, :, 512:1024]]
            last_sil = {0: 0, 1: 0}
            last_hev = {0: 0, 1: 0}
            for e in range(9):
                for fh in range(2):
                    rnd = e * 2 + fh
                    wpar = rnd % 2
                    tdv = p.inc("d_td" + str(wpar), 16 * 8)
                    def gtd(E, e=e, fh=fh, wpar=wpar, tdv=tdv, castv=castv,
                            guard=(d_cons[-1][1] if d_cons else 0)):
                        if e == 0 and fh == 0:
                            E.wait_ge(S("d_castA"), castv)
                        if guard:
                            E.wait_ge(S("c_t"), guard)
                        for fc in range(8):
                            fcg = fh * 8 + fc
                            E.dma_start(out=wdThb[wpar][:, fc, :],
                                        in_=d["wd_bf9"][e, :, ts(fcg)],
                                        transpose=True).then_inc(S("d_td" + str(wpar)), 16)
                    p.on("y", gtd)

                    for qq in range(2):
                        iq = rnd * 2 + qq
                        slot = iq % 2
                        tqv = p.inc("d_tq" + str(slot), 16 * 16)
                        def gtq(E, e=e, fh=fh, qq=qq, slot=slot, tqv=tqv,
                                guard=(q_cons[-2][1] if len(q_cons) >= 2 else 0)):
                            if guard:
                                E.wait_ge(S("c_t"), guard)
                            fq = fh * 1024 + qq * 512
                            for dc in range(DCH):
                                E.dma_start(out=gbufs[slot][:, dc, :],
                                            in_=d["wg_bf9"][e, fq:fq + 512, ts(dc)],
                                            transpose=True).then_inc(S("d_tq" + str(slot)), 16)
                                E.dma_start(out=ubufs[slot][:, dc, :],
                                            in_=d["wu_bf9"][e, fq:fq + 512, ts(dc)],
                                            transpose=True).then_inc(S("d_tq" + str(slot)), 16)
                        p.on("y", gtq)

                        last_t = 0
                        for ft in range(4):
                            fc_in_half = qq * 4 + ft
                            for mh in range(2):
                                sil_prev = last_sil[mh]
                                hev_prev = last_hev[mh]
                                t0 = p.inc("c_t", 1)
                                def tg(E, ft=ft, mh=mh, slot=slot,
                                       tqv=tqv, sp=sil_prev, hp=hev_prev,
                                       first=(ft == 0 and mh == 0)):
                                    if first:
                                        E.wait_ge(S("d_tq" + str(slot)), tqv)
                                    if sp:
                                        E.wait_ge(S("c_s"), sp)
                                    if hp:
                                        E.wait_ge(S("c_v"), hp)
                                    fcol = ft * 128
                                    for dc in range(DCH):
                                        for j in range(2):
                                            E.matmul(psA[:, mh * 1024 + j * 512:
                                                          mh * 1024 + (j + 1) * 512],
                                                     gbufs[slot][:, dc, fcol:fcol + 128],
                                                     xnT[:, dc, mh * 1024 + j * 512:
                                                         mh * 1024 + (j + 1) * 512],
                                                     start=(dc == 0), stop=(dc == DCH - 1))
                                    for dc in range(DCH):
                                        for j in range(2):
                                            E.matmul(psBC[mh][:, ts(j, 512)],
                                                     ubufs[slot][:, dc, fcol:fcol + 128],
                                                     xnT[:, dc, mh * 1024 + j * 512:
                                                         mh * 1024 + (j + 1) * 512],
                                                     start=(dc == 0), stop=(dc == DCH - 1))
                                    E.last_inc(S("c_t"), 1)
                                p.on("t", tg)
                                last_t = t0
                                s1 = p.inc("c_s", 1)
                                def sg(E, mh=mh, t0=t0):
                                    E.wait_ge(S("c_t"), t0)
                                    E.activation(silu_g[mh][:, :],
                                                 psA[:, ts(mh, 1024)], AF.Silu)
                                    E.last_inc(S("c_s"), 1)
                                p.on("s", sg)
                                last_sil[mh] = s1
                                v0 = p.inc("c_v", 1)
                                def vh(E, fc=fc_in_half, mh=mh, s1=s1):
                                    E.wait_ge(S("c_s"), s1)
                                    E.tensor_tensor(hTh[:, fc, ts(mh, 1024)],
                                                    silu_g[mh][:, :],
                                                    psBC[mh][:, :], ALU.mult)
                                    E.last_inc(S("c_v"), 1)
                                p.on("v", vh)
                                last_hev[mh] = v0
                        q_cons.append(((e, iq), last_t))

                    h_done = p.val("c_v")
                    dn_sig = [0] * NCH
                    last_dn = 0
                    for mt in range(NCH):
                        t0 = p.inc("c_t", 1)
                        def td(E, mt=mt, wpar=wpar, tdv=tdv, hd=h_done,
                               guard=(dn_sig[mt - 2] if mt >= 2 else 0)):
                            if mt == 0:
                                E.wait_ge(S("d_td" + str(wpar)), tdv)
                                E.wait_ge(S("c_v"), hd)
                            if guard:
                                E.wait_ge(S("c_v"), guard)
                            ps = psBC[mt % 2]
                            for fc in range(8):
                                for j in range(2):
                                    E.matmul(ps[:, ts(j, 512)],
                                             hTh[:, fc, ts(mt)],
                                             wdThb[wpar][:, fc, ts(j, 512)],
                                             start=(fc == 0), stop=(fc == 7))
                            E.last_inc(S("c_t"), 1)
                        p.on("t", td)
                        last_dn = t0
                        v0 = p.inc("c_v", 1)
                        def vo(E, e=e, fh=fh, mt=mt, t0=t0, combv=combv,
                               prev=(oacc_sig[rnd - 1][mt] if rnd >= 1 else 0)):
                            E.wait_ge(S("c_t"), t0)
                            if e == 1 and fh == 0 and mt == 0:
                                E.wait_ge(S("c_v"), combv)
                            if prev:
                                E.wait_ge(S("c_v"), prev)
                            ps = psBC[mt % 2]
                            if e == 0 and fh == 0:
                                E.tensor_copy(o_acc[:, mt, :], ps[:, :])
                            elif e == 0:
                                E.tensor_add(o_acc[:, mt, :], o_acc[:, mt, :], ps[:, :])
                            else:
                                E.scalar_tensor_tensor(o_acc[:, mt, :], ps[:, :],
                                                       comb[:, mt, e - 1:e],
                                                       o_acc[:, mt, :],
                                                       ALU.mult, ALU.add)
                            E.last_inc(S("c_v"), 1)
                        p.on("v", vo)
                        dn_sig[mt] = v0
                        oacc_sig[rnd][mt] = v0
                    last_hev[0] = dn_sig[14]
                    last_hev[1] = dn_sig[15]
                    d_cons.append(((e, fh), last_dn))
            if "oacc" in TAPS:
                vv = p.val("c_v")
                p.inc("d_tap", 16)
                def gt(E, vv=vv):
                    E.wait_ge(S("c_v"), vv)
                    E.dma_start(out=d["tap_oacc"].rearrange("(n p) e -> p n e", p=128),
                                in_=o_acc[:, :, :]).then_inc(S("d_tap"), 16)
                p.on("g", gt)

        def p9():
            ov = p.val("c_v")
            out_sig = [0] * NCH
            vf_sig = [0] * NCH
            for mt in range(NCH):
                par = mt % 2
                sname = "d_ld" + str(par)
                lv = p.inc(sname, 32)
                def gld(E, mt=mt, par=par, sname=sname,
                        pw=(((("d_x10", p.val("d_x10")), ("d_x11", p.val("d_x11")),
                              ("d_s10", p.val("d_s10")), ("d_s11", p.val("d_s11"))))
                            if mt < 2 else ()),
                        guard=(out_sig[mt - 2] if mt >= 2 else 0)):
                    for nm, vv in pw:
                        E.wait_ge(S(nm), vv)
                    if guard:
                        E.wait_ge(S("d_out" + str(par)), guard)
                    E.dma_start(out=x9ch[par][:, :],
                                in_=d["x1_hbm"][ts(mt), :]).then_inc(S(sname), 16)
                    E.dma_start(out=s9ch[par][:, :],
                                in_=d["s1_hbm"][ts(mt), :]).then_inc(S(sname), 16)
                p.on("g", gld)
                s0 = p.inc("c_s", 1)
                def sg(E, mt=mt, par=par, ov=ov,
                       guard=(vf_sig[mt - 2] if mt >= 2 else 0)):
                    E.wait_ge(S("c_v"), ov)
                    if guard:
                        E.wait_ge(S("c_v"), guard)
                    E.activation(g1ch[par][:, :], o_acc[:, mt, :], AF.Silu)
                    E.last_inc(S("c_s"), 1)
                p.on("s", sg)
                v0 = p.inc("c_v", 1)
                def vf(E, mt=mt, par=par, s0=s0, lv=lv, sname=sname):
                    E.wait_ge(S("c_s"), s0)
                    E.wait_ge(S(sname), lv)
                    E.tensor_tensor(o9ch[par][:, :], s9ch[par][:, :],
                                    g1ch[par][:, :], ALU.mult)
                    E.drain()
                    E.tensor_add(o9ch[par][:, :], x9ch[par][:, :], o9ch[par][:, :])
                    E.last_inc(S("c_v"), 1)
                p.on("v", vf)
                vf_sig[mt] = v0
                osv = p.inc("d_out" + str(par), 16)
                def gst(E, mt=mt, par=par, v0=v0):
                    E.wait_ge(S("c_v"), v0)
                    E.dma_start(out=d["out"][ts(mt), :],
                                in_=o9ch[par][:, :]).then_inc(S("d_out" + str(par)), 16)
                p.on("y", gst)
                out_sig[mt] = osv

        p0()
        p.barrier()
        if upto >= 1:
            p1()
            p.barrier()
        if upto >= 2:
            p2()
            p.barrier()
        if upto >= 3:
            attention_l0()
            p.barrier()
        if upto >= 4:
            p4()
            p.barrier()
        if upto >= 5:
            p5()
            p.barrier()
        if upto >= 6:
            attention(1)
            p.barrier()
        if upto >= 7:
            p7()
            p.barrier()
        if upto >= 8:
            p8()
            p.barrier()
        if upto >= 9:
            p9()
            p.barrier()

        def fin():
            waits = []
            for nm in ("d_out0", "d_out1", "d_tap"):
                if nm in p.cnt and p.cnt[nm]:
                    waits.append((nm, p.cnt[nm]))
            def y(E, waits=tuple(waits)):
                for nm, vv in waits:
                    E.wait_ge(S(nm), vv)
            p.on("y", y)
        fin()

        with nc.Block() as block:
            @block.gpsimd
            def _(E):
                p.replay("g", E)

            @block.tensor
            def _(E):
                p.replay("t", E)

            @block.scalar
            def _(E):
                p.replay("s", E)

            @block.vector
            def _(E):
                p.replay("v", E)

            @block.sync
            def _(E):
                p.replay("y", E)

    return nc


def make_inputs(inp):
    f = np.ascontiguousarray
    B = 8
    x = np.asarray(inp["L0_attn_x"], dtype=np.float32)
    wq = f(np.asarray(inp["L0_attn_Wq"]).reshape(D, D))
    wk = f(np.asarray(inp["L0_attn_Wk"]).reshape(D, D))
    wv = f(np.asarray(inp["L0_attn_Wv"]).reshape(D, D))
    wo = f(np.asarray(inp["L0_attn_Wo"]).reshape(D, D))
    bq = np.asarray(inp["L0_attn_bq"]).reshape(D)
    bk = np.asarray(inp["L0_attn_bk"]).reshape(D)
    bv = np.asarray(inp["L0_attn_bv"]).reshape(D)
    bo = np.asarray(inp["L0_attn_bo"]).reshape(D)
    gam = np.asarray(inp["L0_attn_gamma"]).reshape(D)
    bet = np.asarray(inp["L0_attn_beta"]).reshape(D)
    phi2 = f(np.asarray(inp["L0_ffn_phi"]).reshape(D, NP))
    w0 = f(np.asarray(inp["L0_ffn_W"]))
    b0 = np.asarray(inp["L0_ffn_b"])
    wqkv = np.asarray(inp["L1_attn_W_qkv"]).reshape(D, NH, 3, 64)
    bqkv = np.asarray(inp["L1_attn_b_qkv"]).reshape(NH, 3, 64)
    wout = f(np.asarray(inp["L1_attn_W_out"]))
    bout = np.asarray(inp["L1_attn_b_out"]).reshape(D)
    rw = f(np.asarray(inp["L1_ffn_router_w"]))
    ebias = np.asarray(inp["L1_ffn_expert_bias"]).reshape(NE)
    swg = f(np.asarray(inp["L1_ffn_sw_gate"]))
    swu = f(np.asarray(inp["L1_ffn_sw_up"]))
    swd = f(np.asarray(inp["L1_ffn_sw_down"]))
    ewg = f(np.asarray(inp["L1_ffn_ew_gate"]))
    ewu = f(np.asarray(inp["L1_ffn_ew_up"]))
    ewd = f(np.asarray(inp["L1_ffn_ew_down"]))

    col = lambda a: f(a.reshape(8, 128).T)
    bqkcols = np.concatenate(
        [col(bq), col(bk),
         col(f(bqkv[:, 0, :].reshape(D))), col(f(bqkv[:, 1, :].reshape(D)))],
        axis=1).astype(np.float32)
    brows = np.stack([bv, bo, f(bqkv[:, 2, :].reshape(D)), bout]).astype(np.float32)
    gamma_bc = f(np.broadcast_to(gam[None, :], (128, D)).copy())
    beta_bc = f(np.broadcast_to(bet[None, :], (128, D)).copy())
    b0e = f(np.repeat(b0, 32, axis=0))
    ebias_bc = f(np.broadcast_to(ebias[None, :], (128, NE)).copy())
    wq1 = f(wqkv[:, :, 0, :].reshape(D, D))
    wk1 = f(wqkv[:, :, 1, :].reshape(D, D))
    wv1 = f(wqkv[:, :, 2, :].reshape(D, D))

    shared = dict(wq=wq, wk=wk, wv=wv, wo=wo, bqkc=bqkcols, brows=brows,
                  gamma_bc=gamma_bc, beta_bc=beta_bc, phi2=phi2, w0=w0,
                  b0e=b0e, wq1=wq1, wk1=wk1, wv1=wv1, wout=wout, rw=rw,
                  ebias_bc=ebias_bc, swg=swg, swu=swu, swd=swd,
                  ewg=ewg, ewu=ewu, ewd=ewd)
    in_maps = []
    for c in range(B):
        m = dict(shared)
        m["x"] = f(x[c])
        in_maps.append(m)
    return in_maps


def kernel(**inputs):
    in_maps = make_inputs(inputs)
    nc = build()
    res = run_bass_kernel_spmd(nc, in_maps, core_ids=list(range(8)))
    return np.stack([np.asarray(r["out"]) for r in res.results], axis=0)
